# version 37
# baseline (speedup 1.0000x reference)
import numpy as np
import ml_dtypes
import concourse.bass as bass
import concourse.mybir as mybir
from concourse.bass_utils import run_bass_kernel_spmd

F32 = mybir.dt.float32
BF16 = mybir.dt.bfloat16
I32 = mybir.dt.int32
AF = mybir.ActivationFunctionType
ALU = mybir.AluOpType
AX = mybir.AxisListType

NB = 2
T = 4096
D = 1024
DI = 2048
NDT = DI // 128
DS = 16
RK = 64
DC = 4
HID = 4096
EPS = 1e-5

Q1 = 256
TT1 = Q1 // 128
NCH1 = T // Q1
NG = 2
Q2 = 512
TT2 = Q2 // 128
NCH2 = T // Q2

ENGS = ("pe", "act", "dve", "pool", "sp")
DMAS = ("ld", "st")
_DBG = {"on": False}


class Buf:
    __slots__ = ("w", "readers")

    def __init__(self):
        self.w = None
        self.readers = []

    def rd(self):
        return [self.w] if self.w is not None else []

    def wr(self):
        d = [self.w] if self.w is not None else []
        return d + self.readers

    def setw(self, tok):
        self.w = tok
        self.readers = []

    def addr(self, tok):
        self.readers.append(tok)


class Em:
    def __init__(self):
        self.q = {e: [] for e in ENGS}
        self.cnt = {e: 0 for e in ENGS}
        self.dcnt = {k: 0 for k in DMAS}

    def step(self, eng, fn, deps=()):
        deps = [t for t in deps if t is not None]
        self.cnt[eng] += 1
        tok = (eng, self.cnt[eng])
        self.q[eng].append(("c", eng, deps, fn))
        return tok

    def dma(self, queue, sem, dst, src, deps=()):
        deps = [t for t in deps if t is not None]
        self.dcnt[sem] += 16
        tok = (sem, self.dcnt[sem])
        self.q[queue].append(("d", sem, deps, (dst, src)))
        return tok

    def join(self):
        t = [(e, self.cnt[e]) for e in ENGS if self.cnt[e] > 0]
        t += [(k, self.dcnt[k]) for k in DMAS if self.dcnt[k] > 0]
        return t

    def emit(self, block, sems):
        def mk(eng_name):
            def body(eng):
                wm = {}
                for kind, key, deps, payload in self.q[eng_name]:
                    best = {}
                    for (sk, sv) in deps:
                        if wm.get(sk, 0) < sv and best.get(sk, 0) < sv:
                            best[sk] = sv
                    for sk in sorted(best):
                        eng.wait_ge(sems[sk], best[sk])
                        wm[sk] = best[sk]
                    if kind == "c":
                        payload(eng).then_inc(sems[key], 1)
                    else:
                        dst, src = payload
                        eng.dma_start(dst, src).then_inc(sems[key], 16)
            return body
        block.tensor(mk("pe"))
        block.scalar(mk("act"))
        block.vector(mk("dve"))
        block.gpsimd(mk("pool"))
        block.sync(mk("sp"))


class Arena:
    def __init__(self, nc, name, cols):
        self.t = nc.alloc_sbuf_tensor(name, [128, cols], BF16)
        self.cols = cols
        self.off = 0

    def take(self, n_elem, dt):
        unit = {F32: 2, BF16: 1, I32: 2}[dt]
        self.off = (self.off + 1) & ~1
        o = self.off
        self.off += n_elem * unit
        assert self.off <= self.cols, f"arena overflow {self.off} > {self.cols}"
        ap = self.t[:, o:o + n_elem * unit]
        if dt != BF16:
            ap = ap.bitcast(dt)
        return ap

    def mark(self):
        return self.off

    def reset(self, off=0):
        self.off = off


def build_program():
    nc = bass.Bass("TRN2", target_bir_lowering=False)

    x_in = nc.dram_tensor("L0_attn_x", [NB, T, D], F32, kind="ExternalInput")
    w_inproj = nc.dram_tensor("L0_attn_in_proj_w", [D, 2 * DI], F32, kind="ExternalInput")
    w_conv = nc.dram_tensor("L0_attn_conv_w", [DI, 1, DC], F32, kind="ExternalInput")
    b_conv = nc.dram_tensor("L0_attn_conv_b", [DI], F32, kind="ExternalInput")
    w_xproj = nc.dram_tensor("L0_attn_x_proj_w", [DI, RK + 2 * DS], F32, kind="ExternalInput")
    w_dtp = nc.dram_tensor("L0_attn_dt_proj_w", [RK, DI], F32, kind="ExternalInput")
    b_dt = nc.dram_tensor("L0_attn_dt_proj_b", [DI], F32, kind="ExternalInput")
    a_log = nc.dram_tensor("L0_attn_A_log", [DI, DS], F32, kind="ExternalInput")
    dp_in = nc.dram_tensor("L0_attn_Dp", [DI], F32, kind="ExternalInput")
    w_out = nc.dram_tensor("L0_attn_out_proj_w", [DI, D], F32, kind="ExternalInput")
    w_ffn = {
        "g": nc.dram_tensor("L0_ffn_w_gate", [HID, D], F32, kind="ExternalInput"),
        "u": nc.dram_tensor("L0_ffn_w_up", [HID, D], F32, kind="ExternalInput"),
        "d": nc.dram_tensor("L0_ffn_w_down", [D, HID], F32, kind="ExternalInput"),
    }
    c_eye = nc.dram_tensor("c_eye", [128, 128], BF16, kind="ExternalInput")
    c_sel = nc.dram_tensor("c_sel", [32, 32 * 128], BF16, kind="ExternalInput")
    c_onesrow = nc.dram_tensor("c_onesrow", [1, 128], F32, kind="ExternalInput")
    c_onesb = nc.dram_tensor("c_onesb", [128, 1], BF16, kind="ExternalInput")
    c_cols = nc.dram_tensor("c_cols", [128, 2], F32, kind="ExternalInput")

    out_d = nc.dram_tensor("out", [NB, T, D], F32, kind="ExternalOutput")
    if _DBG["on"]:
        dbg_xq = nc.dram_tensor("dbg_xq", [TT2, 128, D], BF16, kind="ExternalOutput")
        dbg_gu = nc.dram_tensor("dbg_gu", [TT2, 128, HID], F32, kind="ExternalOutput")
        dbg_sc = nc.dram_tensor("dbg_sc", [2, 128, 16], F32, kind="ExternalOutput")
    xmid_d = nc.dram_tensor("xmid", [NB, T, D], F32, kind="Internal")
    wqt_d = {
        "g": nc.dram_tensor("wqtg", [D, HID], BF16, kind="Internal"),
        "u": nc.dram_tensor("wqtu", [D, HID], BF16, kind="Internal"),
        "d": nc.dram_tensor("wqtd", [HID, D], BF16, kind="Internal"),
    }

    em = Em()
    bufs = {}

    def B(name):
        if name not in bufs:
            bufs[name] = Buf()
        return bufs[name]

    eye = nc.alloc_sbuf_tensor("eye", [128, 128], BF16)
    onesrow = nc.alloc_sbuf_tensor("onesrow", [1, 128], F32)
    onesb = nc.alloc_sbuf_tensor("onesb", [128, 1], BF16)
    cm_c = nc.alloc_sbuf_tensor("cm_c", [128, 4], F32)
    stat = nc.alloc_sbuf_tensor("stat", [128, 64], F32)
    statb = nc.alloc_sbuf_tensor("statb", [128, 48], F32)
    ccols = nc.alloc_sbuf_tensor("ccols", [128, 2], F32)

    ar = Arena(nc, "arena", 105700)

    p_tr = nc.alloc_psum_tensor("p_tr", [128, 512], BF16)
    p_tr2 = nc.alloc_psum_tensor("p_tr2", [128, 512], BF16)
    trrot = {"i": 0}

    def tr_bank():
        i = trrot["i"] % 2
        trrot["i"] += 1
        return (p_tr, p_tr2)[i], f"ptr{i}"
    p_mm0 = nc.alloc_psum_tensor("p_mm0", [128, 512], F32)
    p_mm1 = nc.alloc_psum_tensor("p_mm1", [128, 512], F32)
    p_dbc = nc.alloc_psum_tensor("p_dbc", [128, 512], F32)
    p_dt = nc.alloc_psum_tensor("p_dt", [128, 512], F32)
    p_out = nc.alloc_psum_tensor("p_out", [128, 1024], F32)

    win = [ar.take(2 * DI, BF16) for _ in range(D // 128)]
    wout = [ar.take(D, BF16) for _ in range(NDT)]
    wxp = [ar.take(RK + 2 * DS, BF16) for _ in range(NDT)]
    sel = ar.take(32 * 128, BF16)
    wdt = sel[:, 0:DI]
    a_cols = ar.take(NDT * DS, F32)
    convw = ar.take(NDT * DC, F32)
    convb_c = ar.take(NDT, F32)
    dtb_c = ar.take(NDT, F32)
    dp_c = ar.take(NDT, F32)
    hcar = ar.take(NDT * DS, F32)
    uhalo = ar.take(NDT * (DC - 1), BF16)
    p1_cb = ar.mark()

    xln_st = [ar.take(D, BF16)]
    bbig = ar.take(DS * Q1, BF16)
    cbig = ar.take(DS * Q1, BF16)
    xlnT = ar.take((D // 128) * Q1, BF16)
    x_pairs = [ar.take(TT1 * D, F32) for _ in range(2)]
    u_sc = [ar.take(DC - 1 + Q1 + 1, BF16) for _ in range(2)]
    uc_b = ar.take(NDT * Q1, BF16)
    zs_b = ar.take(NDT * Q1, BF16)
    dbc_b = ar.take(Q1, BF16)
    dA_bufs = [ar.take(DS * Q1, BF16) for _ in range(2)]
    X_big = ar.take(DS * Q1, BF16)
    H_big = ar.take(DS * Q1, BF16)
    tree = H_big.bitcast(F32)
    dt_sc = ar.take(Q1, F32)
    a_sc = ar.take(Q1, F32)
    dtu_bufs = [ar.take(Q1, BF16) for _ in range(2)]
    y_sc = ar.take(Q1, F32)
    cacc = ar.take(Q1, F32)
    cacc2 = ar.take(Q1, F32)
    car16 = ar.take(DS, F32)
    p1_end = ar.mark()

    ar.reset(p1_cb)
    stage = [ar.take(4096, F32) for _ in range(2)]
    wq_i = ar.take(HID, I32)
    wq_b = ar.take(HID, BF16)
    abs_b = ar.take(HID, BF16)
    wqt_s = ar.take(512, BF16)
    ar.reset(p1_end)

    tok_consts = [
        em.dma("sp", "ld", eye[:, :], c_eye[:, :]),
        em.dma("sp", "ld", sel[64:96, :], c_sel[:, :]),
        em.dma("sp", "ld", onesrow[:, :], c_onesrow[:, :]),
        em.dma("sp", "ld", onesb[:, :], c_onesb[:, :]),
        em.dma("sp", "ld", ccols[:, :], c_cols[:, :]),
    ]
    prep_last = list(tok_consts)

    def load_convert(dram, rows, cols, dsts):
        for k in range(rows // 128):
            st = stage[k % 2]
            sb = B(f"stg{k % 2}")
            if cols >= 1024:
                h = cols // 2
                t_l0 = em.dma("sp", "ld", st[:, :h],
                              dram[k * 128:(k + 1) * 128, 0:h], deps=sb.wr())
                t_ld = em.dma("sp", "ld", st[:, h:cols],
                              dram[k * 128:(k + 1) * 128, h:cols], deps=[])
                sb.setw(t_ld)
                sb.addr(t_l0)
            else:
                t_ld = em.dma("sp", "ld", st[:, :cols],
                              dram[k * 128:(k + 1) * 128, :], deps=sb.wr())
                sb.setw(t_ld)
            t_cp = em.step("dve", lambda e, st=st, k=k, cols=cols, dsts=dsts:
                           e.tensor_copy(dsts[k][:, :cols], st[:, :cols]),
                           deps=sb.rd())
            sb.addr(t_cp)
            prep_last.append(t_cp)

    load_convert(w_inproj, D, 2 * DI, win)
    load_convert(w_out, DI, D, wout)
    load_convert(w_xproj, DI, RK + 2 * DS, wxp)

    sb = B("stg0")
    t_ld = em.dma("sp", "ld", stage[0][0:RK, :DI], w_dtp[:, :], deps=sb.wr())
    sb.setw(t_ld)
    t_cp = em.step("dve", lambda e: e.tensor_copy(wdt[0:RK, :], stage[0][0:RK, :DI]),
                   deps=sb.rd())
    sb.addr(t_cp)
    prep_last.append(t_cp)

    for k in range(NDT):
        st = stage[k % 2]
        sb = B(f"stg{k % 2}")
        t_ld = em.dma("sp", "ld", st[:, :DS], a_log[k * 128:(k + 1) * 128, :],
                      deps=sb.wr())
        sb.setw(t_ld)
        t_e = em.step("act", lambda e, st=st: e.activation(
            st[:, 64:64 + DS], st[:, :DS], AF.Exp, bias=0.0, scale=1.0),
            deps=sb.rd())
        sb.addr(t_e)
        sb.setw(t_e)
        t_n = em.step("dve", lambda e, st=st, k=k: e.tensor_scalar_mul(
            a_cols[:, k * DS:(k + 1) * DS], st[:, 64:64 + DS], -1.0), deps=[t_e])
        sb.addr(t_n)
        prep_last.append(t_n)

    wc2 = w_conv.rearrange("d one c -> d (one c)")
    bc2 = b_conv.rearrange("(a b) -> a b", b=1)
    bd2 = b_dt.rearrange("(a b) -> a b", b=1)
    dp2 = dp_in.rearrange("(a b) -> a b", b=1)
    for k in range(NDT):
        prep_last.append(em.dma("sp", "ld", convw[:, k * DC:(k + 1) * DC],
                                wc2[k * 128:(k + 1) * 128, :]))
        prep_last.append(em.dma("sp", "ld", convb_c[:, k:k + 1],
                                bc2[k * 128:(k + 1) * 128, :]))
        prep_last.append(em.dma("sp", "ld", dtb_c[:, k:k + 1],
                                bd2[k * 128:(k + 1) * 128, :]))
        prep_last.append(em.dma("sp", "ld", dp_c[:, k:k + 1],
                                dp2[k * 128:(k + 1) * 128, :]))

    def ffn_quant(wdram, rows, cols, outdram, cm_idx, tag):
        ntile = rows // 128
        pa = B("pdbc")
        ab = B("absb")
        for r in range(ntile):
            st = stage[r % 2]
            sb = B(f"stg{r % 2}")
            h = cols // 2
            t_l0 = em.dma("sp", "ld", st[:, :h],
                          wdram[r * 128:(r + 1) * 128, 0:h], deps=sb.wr())
            t_ld = em.dma("sp", "ld", st[:, h:cols],
                          wdram[r * 128:(r + 1) * 128, h:cols], deps=[])
            sb.setw(t_ld)
            sb.addr(t_l0)
            t_ab = em.step("act", lambda e, st=st, cols=cols: e.activation(
                abs_b[:, :cols], st[:, :cols], AF.Abs, bias=0.0, scale=1.0),
                deps=sb.rd() + ab.wr())
            sb.addr(t_ab)
            ab.setw(t_ab)

            nh = (cols + 511) // 512

            def mmfn(e, r=r, cols=cols, ntile=ntile, nh=nh):
                last = None
                for h in range(nh):
                    w = min(512, cols - h * 512)
                    last = e.matmul(p_dbc[0:1, 0:w], onesb[:, 0:1],
                                    abs_b[:, h * 512:h * 512 + w],
                                    start=(r == 0 and h == 0),
                                    stop=(r == ntile - 1 and h == nh - 1))
                return last
            t_mm = em.step("pe", mmfn,
                           deps=ab.rd() + tok_consts + (pa.wr() if r == 0 else []))
            ab.addr(t_mm)
            if r == 0:
                pa.setw(t_mm)
            else:
                pa.addr(t_mm)
        msb = B("statb")
        t_rs = em.step("dve", lambda e: e.reduce_sum(
            statb[0:1, 0:1], p_dbc[0:1, :512], AX.X), deps=pa.wr() + msb.wr())
        pa.addr(t_rs)
        msb.setw(t_rs)
        t_mean = em.step("dve", lambda e, rows=rows, cols=cols: e.tensor_scalar(
            statb[0:1, 1:2], statb[0:1, 0:1], 1.0 / (rows * cols), 1e-5,
            ALU.mult, ALU.max), deps=[t_rs])
        msb.addr(t_mean)
        pb = B("pdt")
        t_bc = em.step("pe", lambda e: e.matmul(
            p_dt[:, 0:1], onesrow[0:1, :], statb[0:1, 1:2], start=True, stop=True),
            deps=[t_mean] + pb.wr() + tok_consts)
        msb.addr(t_bc)
        pb.setw(t_bc)
        cmb = B("cm")
        t_cm = em.step("act", lambda e, cm_idx=cm_idx: e.copy(
            cm_c[:, cm_idx:cm_idx + 1], p_dt[:, 0:1]), deps=pb.rd() + cmb.wr())
        pb.addr(t_cm)
        cmb.setw(t_cm)
        scb = B("wscale")
        t_sc = em.step("dve", lambda e, cm_idx=cm_idx: e.reciprocal(
            statb[:, 2:3], cm_c[:, cm_idx:cm_idx + 1]),
            deps=[t_cm] + scb.wr() + [t_mean])
        cmb.addr(t_sc)
        msb.addr(t_sc)
        scb.setw(t_sc)
        for r in range(ntile):
            st = stage[r % 2]
            sb = B(f"stg{r % 2}")
            h = cols // 2
            t_l0 = em.dma("sp", "ld", st[:, :h],
                          wdram[r * 128:(r + 1) * 128, 0:h], deps=sb.wr())
            t_ld = em.dma("sp", "ld", st[:, h:cols],
                          wdram[r * 128:(r + 1) * 128, h:cols], deps=[])
            sb.setw(t_ld)
            sb.addr(t_l0)
            qi = B("wqi")
            t_q = em.step("act", lambda e, st=st, cols=cols: e.activation(
                wq_i[:, :cols], st[:, :cols], AF.Copy, bias=0.0, scale=statb[:, 2:3]),
                deps=sb.rd() + qi.wr() + scb.rd())
            sb.addr(t_q)
            scb.addr(t_q)
            qi.setw(t_q)
            qb = B("wqb")
            t_c = em.step("dve", lambda e, cols=cols: e.tensor_scalar(
                wq_b[:, :cols], wq_i[:, :cols], 1.0, -1.0, ALU.min, ALU.max),
                deps=qi.rd() + qb.wr())
            qi.addr(t_c)
            qb.setw(t_c)
            for g in range(cols // 512):
                ptr_t, ptr_n = tr_bank()
                ptb = B(ptr_n)
                tts = []
                for j in range(4):
                    db = g * 4 + j
                    t_t = em.step("pe", lambda e, db=db, j=j, ptr_t=ptr_t: e.transpose(
                        ptr_t[:, j * 128:(j + 1) * 128],
                        wq_b[:, db * 128:(db + 1) * 128], eye[:, :]),
                        deps=qb.rd() + tok_consts + (ptb.wr() if j == 0 else []))
                    qb.addr(t_t)
                    if j == 0:
                        ptb.setw(t_t)
                    tts.append(t_t)
                ws = B("wqts")
                t_e = em.step("act", lambda e, ptr_t=ptr_t: e.copy(
                    wqt_s[:, :], ptr_t[:, :]), deps=tts + ws.wr())
                ptb.addr(t_e)
                ws.setw(t_e)
                dst = outdram.rearrange("(a p) c -> p a c", p=128)[
                    :, g * 4:(g + 1) * 4, r * 128:(r + 1) * 128]
                src = wqt_s[:, :].rearrange("p (a c) -> p a c", a=4)
                t_st = em.dma("pool", "st", dst, src, deps=ws.rd())
                ws.addr(t_st)
                prep_last.append(t_st)

    ffn_quant(w_ffn["g"], HID, D, wqt_d["g"], 0, "g")
    ffn_quant(w_ffn["u"], HID, D, wqt_d["u"], 1, "u")
    ffn_quant(w_ffn["d"], D, HID, wqt_d["d"], 2, "d")

    barrier = {"v": prep_last + em.join()}

    def bar():
        return barrier["v"]

    xmid_tok = [[None] * NCH1 for _ in range(NB)]

    for b in range(NB):
        t_i1 = em.step("dve", lambda e: e.memset(hcar[:, :], 0.0),
                       deps=B("hcar").wr() + bar())
        B("hcar").setw(t_i1)
        t_i2 = em.step("dve", lambda e: e.memset(uhalo[:, :], 0.0),
                       deps=B("uhalo").wr() + bar())
        B("uhalo").setw(t_i2)
        for d in range(NDT):
            B(f"uh{d}").setw(t_i2)

        def head_a(c):
            t0 = c * Q1
            x_pair = x_pairs[c % 2]
            x_sb = [x_pair[:, tt * D:(tt + 1) * D] for tt in range(TT1)]
            xb = B(f"xch{c % 2}")
            t_x = em.dma("sp", "ld", x_pair[:, :].rearrange(
                "p (a c) -> p a c", a=TT1),
                x_in.rearrange("b (a p) c -> b p a c", p=128)[
                    b, :, c * TT1:(c + 1) * TT1, :],
                deps=xb.wr() + bar())
            xb.setw(t_x)

            for tt in range(TT1):
                st8 = stat[:, tt * 16:tt * 16 + 16]
                t_s1 = em.step("dve", lambda e, tt=tt, st8=st8, x_sb=x_sb: e.reduce_sum(
                    st8[:, 0:1], x_sb[tt][:, :], AX.X), deps=xb.rd())
                xb.addr(t_s1)
                sqb = B("ysc")
                sqt = []
                for qq in range(4):
                    t_s2 = em.step("act", lambda e, tt=tt, st8=st8, x_sb=x_sb,
                                   qq=qq: e.activation(
                        y_sc[:, :], x_sb[tt][:, qq * 256:(qq + 1) * 256],
                        AF.Square, bias=0.0, scale=1.0,
                        accum_out=st8[:, 8 + qq:9 + qq]),
                        deps=xb.rd() + sqb.wr() + [t_s1] + bar())
                    xb.addr(t_s2)
                    sqb.setw(t_s2)
                    sqt.append(t_s2)
                t_q1 = em.step("dve", lambda e, st8=st8: e.tensor_add(
                    st8[:, 12:13], st8[:, 8:9], st8[:, 9:10]), deps=sqt)
                t_q2 = em.step("dve", lambda e, st8=st8: e.tensor_add(
                    st8[:, 13:14], st8[:, 10:11], st8[:, 11:12]), deps=sqt)
                t_s2 = em.step("dve", lambda e, st8=st8: e.tensor_add(
                    st8[:, 1:2], st8[:, 12:13], st8[:, 13:14]),
                    deps=[t_q1, t_q2])
                t_m = em.step("dve", lambda e, st8=st8: e.tensor_scalar_mul(
                    st8[:, 2:3], st8[:, 0:1], 1.0 / D), deps=[t_s1, t_s2])
                t_v1 = em.step("dve", lambda e, st8=st8: e.tensor_scalar_mul(
                    st8[:, 3:4], st8[:, 1:2], 1.0 / D), deps=[t_s2])
                t_v2 = em.step("dve", lambda e, st8=st8: e.tensor_mul(
                    st8[:, 4:5], st8[:, 2:3], st8[:, 2:3]), deps=[t_m])
                t_v3 = em.step("dve", lambda e, st8=st8: e.tensor_sub(
                    st8[:, 5:6], st8[:, 3:4], st8[:, 4:5]), deps=[t_v1, t_v2])
                t_sd = em.step("act", lambda e, st8=st8: e.activation(
                    st8[:, 6:7], st8[:, 5:6], AF.Sqrt, bias=ccols[:, 0:1], scale=1.0),
                    deps=[t_v3])
                t_rst = em.step("dve", lambda e, st8=st8: e.reciprocal(
                    st8[:, 7:8], st8[:, 6:7]), deps=[t_sd])
                xlb = B("xln0")
                t_ap = em.step("dve", lambda e, tt=tt, st8=st8, x_sb=x_sb: e.tensor_scalar(
                    xln_st[0][:, :], x_sb[tt][:, :], st8[:, 2:3],
                    st8[:, 7:8], ALU.subtract, ALU.mult),
                    deps=xb.rd() + [t_m, t_rst] + xlb.wr())
                xb.addr(t_ap)
                xlb.setw(t_ap)
                xtb = B("xlnT")
                for g4 in range(2):
                    ptr_t, ptr_n = tr_bank()
                    ptb = B(ptr_n)
                    tts = []
                    for j in range(4):
                        db = g4 * 4 + j
                        t_t = em.step("pe", lambda e, db=db, j=j, tt=tt,
                                      ptr_t=ptr_t, x_sb=x_sb: e.transpose(
                            ptr_t[:, j * 128:(j + 1) * 128],
                            xln_st[0][:, db * 128:(db + 1) * 128], eye[:, :]),
                            deps=xlb.rd() + tok_consts +
                            (ptb.wr() if j == 0 else []))
                        xlb.addr(t_t)
                        if j == 0:
                            ptb.setw(t_t)
                        tts.append(t_t)
                    dstv = xlnT.rearrange("p (db q) -> p db q", db=D // 128)[
                        :, g4 * 4:(g4 + 1) * 4, tt * 128:(tt + 1) * 128]
                    t_ev = em.step("act", lambda e, dstv=dstv, ptr_t=ptr_t: e.copy(
                        dstv, ptr_t[:, :].rearrange("p (a c) -> p a c", a=4)),
                        deps=tts + (xtb.wr() if (tt == 0 and g4 == 0) else []))
                    ptb.addr(t_ev)
                    if tt == 0 and g4 == 0:
                        xtb.setw(t_ev)
                    else:
                        xtb.addr(t_ev)
            xtb = B("xlnT")
            xlnT_tok = [xtb.w] + list(xtb.readers)

            return x_pair, x_sb, xb

        def head_b(c, x_state):
            t0 = c * Q1
            x_pair, x_sb, xb = x_state
            xtb = B("xlnT")
            xlnT_tok = [xtb.w] + list(xtb.readers)
            for et in range(2 * DI // 128):
                pm_ = (p_mm0, p_mm1)[et % 2]
                pmb = B(f"pmm{et % 2}")

                def mmfn(e, et=et, pm_=pm_):
                    last = None
                    for k in range(D // 128):
                        last = e.matmul(pm_[:, :Q1],
                                        win[k][:, et * 128:(et + 1) * 128],
                                        xlnT[:, k * Q1:(k + 1) * Q1],
                                        start=(k == 0), stop=(k == D // 128 - 1))
                    return last
                t_mm = em.step("pe", mmfn, deps=xlnT_tok + pmb.wr())
                xtb.addr(t_mm)
                pmb.setw(t_mm)

                if et < NDT:
                    d = et
                    us = u_sc[d % 2]
                    usb = B(f"usc{d % 2}")
                    t_ev = em.step("act", lambda e, us=us, pm_=pm_: e.copy(
                        us[:, DC - 1:DC - 1 + Q1], pm_[:, :Q1]),
                        deps=pmb.rd() + usb.wr())
                    pmb.addr(t_ev)
                    usb.setw(t_ev)
                    hb = B(f"uh{d}")
                    t_h = em.step("dve", lambda e, us=us, d=d: e.tensor_copy(
                        us[:, 0:DC - 1], uhalo[:, d * (DC - 1):(d + 1) * (DC - 1)]),
                        deps=hb.rd() + [t_ev])
                    hb.addr(t_h)
                    t_h2 = em.step("dve", lambda e, us=us, d=d: e.tensor_copy(
                        uhalo[:, d * (DC - 1):(d + 1) * (DC - 1)],
                        us[:, Q1:Q1 + DC - 1]), deps=[t_ev, t_h] + hb.wr())
                    hb.setw(t_h2)
                    usb.addr(t_h2)
                    cb = B("cacc")
                    cb2 = B("cacc2")
                    t_a0 = em.step("dve", lambda e, us=us, d=d: e.tensor_scalar_mul(
                        cacc[:, :], us[:, 0:Q1], convw[:, d * DC:d * DC + 1]),
                        deps=[t_ev, t_h] + cb.wr())
                    usb.addr(t_a0)
                    cb.setw(t_a0)
                    t_b0 = em.step("dve", lambda e, us=us, d=d: e.tensor_scalar_mul(
                        cacc2[:, :], us[:, 1:1 + Q1], convw[:, d * DC + 1:d * DC + 2]),
                        deps=[t_ev, t_h, t_h2] + cb2.wr())
                    usb.addr(t_b0)
                    cb2.setw(t_b0)
                    t_a1 = em.step("dve", lambda e, us=us, d=d:
                                   e.scalar_tensor_tensor(
                                       cacc[:, :], us[:, 2:2 + Q1],
                                       convw[:, d * DC + 2:d * DC + 3],
                                       cacc[:, :], ALU.mult, ALU.add),
                                   deps=[t_a0])
                    usb.addr(t_a1)
                    cb.setw(t_a1)
                    t_b1 = em.step("dve", lambda e, us=us, d=d:
                                   e.scalar_tensor_tensor(
                                       cacc2[:, :], us[:, 3:3 + Q1],
                                       convw[:, d * DC + 3:d * DC + 4],
                                       cacc2[:, :], ALU.mult, ALU.add),
                                   deps=[t_b0])
                    usb.addr(t_b1)
                    cb2.setw(t_b1)
                    t_c = em.step("dve", lambda e: e.tensor_add(
                        cacc[:, :], cacc[:, :], cacc2[:, :]),
                        deps=[t_a1, t_b1])
                    cb.setw(t_c)
                    cb2.addr(t_c)
                    ucb = B(f"uc{d}")
                    t_u = em.step("act", lambda e, d=d: e.activation(
                        uc_b[:, d * Q1:(d + 1) * Q1], cacc[:, :], AF.Silu,
                        bias=convb_c[:, d:d + 1], scale=1.0),
                        deps=[t_c] + ucb.wr())
                    cb.addr(t_u)
                    ucb.setw(t_u)
                else:
                    d = et - NDT
                    zb = B(f"zs{d}")
                    t_z = em.step("act", lambda e, d=d, pm_=pm_: e.activation(
                        zs_b[:, d * Q1:(d + 1) * Q1], pm_[:, :Q1], AF.Silu,
                        bias=0.0, scale=1.0), deps=pmb.rd() + zb.wr())
                    pmb.addr(t_z)
                    zb.setw(t_z)

            pdb = B("pdbc")

            def xpfn(e):
                last = None
                for k in range(NDT):
                    last = e.matmul(p_dbc[0:RK + 2 * DS, :Q1], wxp[k][:, :],
                                    uc_b[:, k * Q1:(k + 1) * Q1],
                                    start=(k == 0), stop=(k == NDT - 1))
                return last
            t_xp = em.step("pe", xpfn,
                           deps=sum([B(f"uc{k}").rd() for k in range(NDT)], [])
                           + pdb.wr())
            for k in range(NDT):
                B(f"uc{k}").addr(t_xp)
            pdb.setw(t_xp)
            dlb = B("dbcb")
            t_e1 = em.step("act", lambda e: e.copy(dbc_b[0:RK, :], p_dbc[0:RK, :Q1]),
                           deps=pdb.rd() + dlb.wr())
            pdb.addr(t_e1)
            t_e2 = em.step("act", lambda e: e.copy(dbc_b[64:96, :],
                                                   p_dbc[64:96, :Q1]),
                           deps=pdb.rd() + [t_e1])
            pdb.addr(t_e2)
            dlb.setw(t_e2)
            dlb.addr(t_e1)

            bb = B("bbig")
            cc = B("cbig")
            def emit_bc():
                bb = B("bbig")
                cc = B("cbig")
                i_bc = 0
                for big, base, bufo in ((bbig, 0, bb), (cbig, 16, cc)):
                    evtoks = []
                    wrdeps = bufo.wr()
                    for p2 in range(DS // 2):
                        n0 = 2 * p2
                        par = i_bc % 2
                        i_bc += 1
                        pm2 = (p_mm0, p_mm1)[par]
                        pmb2 = B(f"pmm{par}")

                        def bmfn(e, base=base, n0=n0, pm2=pm2):
                            e.matmul(pm2[:, 0:Q1],
                                     sel[64:96, (base + n0) * 128:(base + n0 + 1) * 128],
                                     dbc_b[64:96, :], start=True, stop=True)
                            return e.matmul(
                                pm2[:, Q1:2 * Q1],
                                sel[64:96, (base + n0 + 1) * 128:(base + n0 + 2) * 128],
                                dbc_b[64:96, :], start=True, stop=True)
                        t_bm = em.step("pe", bmfn, deps=dlb.rd() + pmb2.wr() + tok_consts)
                        dlb.addr(t_bm)
                        pmb2.setw(t_bm)
                        t_be = em.step("act", lambda e, big=big, n0=n0, pm2=pm2: e.copy(
                            big[:, n0 * Q1:(n0 + 2) * Q1], pm2[:, 0:2 * Q1]),
                            deps=[t_bm] + (wrdeps if n0 == 0 else []))
                        pmb2.addr(t_be)
                        evtoks.append(t_be)
                    bufo.setw(evtoks[-1])
                    for t in evtoks[:-1]:
                        bufo.addr(t)

            return emit_bc

        def dtile_loop(c, x_state, emit_bc):
            t0 = c * Q1
            x_pair, x_sb, xb = x_state
            dlb = B("dbcb")
            bb = B("bbig")
            cc = B("cbig")
            Xv = X_big.rearrange("p (n q) -> p n q", n=DS)
            Hv = H_big.rearrange("p (n q) -> p n q", n=DS)

            def front(d):
                dA_big = dA_bufs[d % 2]
                dtu_sc = dtu_bufs[d % 2]
                pdt_t = (p_dt, p_dbc)[d % 2]
                pdt = B(("pdt", "pdbc")[d % 2])
                t_dm = em.step("pe", lambda e, d=d, pdt_t=pdt_t: e.matmul(
                    pdt_t[:, :Q1], wdt[0:RK, d * 128:(d + 1) * 128],
                    dbc_b[0:RK, :], start=True, stop=True),
                    deps=dlb.rd() + pdt.wr())
                dlb.addr(t_dm)
                pdt.setw(t_dm)
                rb = B("dtsc")
                t_r = em.step("act", lambda e, d=d, pdt_t=pdt_t: e.activation(
                    dt_sc[:, :], pdt_t[:, :Q1], AF.Relu, bias=dtb_c[:, d:d + 1],
                    scale=1.0), deps=pdt.rd() + rb.wr())
                pdt.addr(t_r)
                rb.setw(t_r)
                ab_ = B("asc")
                t_a = em.step("act", lambda e, d=d, pdt_t=pdt_t: e.activation(
                    a_sc[:, :], pdt_t[:, :Q1], AF.Abs, bias=dtb_c[:, d:d + 1],
                    scale=1.0), deps=pdt.rd() + ab_.wr())
                pdt.addr(t_a)
                ab_.setw(t_a)
                t_a = em.step("act", lambda e: e.activation(
                    a_sc[:, :], a_sc[:, :], AF.Exp, bias=0.0, scale=-1.0),
                    deps=[t_a])
                ab_.setw(t_a)
                t_a = em.step("act", lambda e: e.activation(
                    a_sc[:, :], a_sc[:, :], AF.Ln, bias=ccols[:, 1:2], scale=1.0),
                    deps=[t_a])
                ab_.setw(t_a)
                t_dt = em.step("pool", lambda e: e.tensor_add(
                    dt_sc[:, :], dt_sc[:, :], a_sc[:, :]),
                    deps=[t_r, t_a] + rb.readers)
                rb.setw(t_dt)
                ab_.addr(t_dt)
                ucb = B(f"uc{d}")
                dub = B(f"dtu{d % 2}")
                t_du = em.step("pool", lambda e, d=d, dtu_sc=dtu_sc: e.tensor_mul(
                    dtu_sc[:, :], dt_sc[:, :], uc_b[:, d * Q1:(d + 1) * Q1]),
                    deps=[t_dt] + ucb.rd() + dub.wr())
                rb.addr(t_du)
                ucb.addr(t_du)
                dub.setw(t_du)
                dab = B(f"dAbig{d % 2}")
                datoks = []
                for n in range(DS):
                    t_da = em.step("act", lambda e, d=d, n=n, dA_big=dA_big:
                                   e.activation(
                                       dA_big[:, n * Q1:(n + 1) * Q1], dt_sc[:, :],
                                       AF.Exp, bias=0.0,
                                       scale=a_cols[:, d * DS + n:d * DS + n + 1]),
                                   deps=[t_dt] + (dab.wr() if n == 0 else []))
                    rb.addr(t_da)
                    if n == 0:
                        dab.setw(t_da)
                    datoks.append(t_da)
                return datoks

            def back(d, datoks):
                dA_big = dA_bufs[d % 2]
                dAv = dA_big.rearrange("p (n q) -> p n q", n=DS)
                dtu_sc = dtu_bufs[d % 2]
                dab = B(f"dAbig{d % 2}")
                dub = B(f"dtu{d % 2}")
                ucb = B(f"uc{d}")
                xbb = B("Xbig")
                t_X = em.step("dve", lambda e, dtu_sc=dtu_sc: e.tensor_tensor(
                    Xv, dtu_sc[:, :].unsqueeze(1).broadcast_to([128, DS, Q1]),
                    bbig[:, :].rearrange("p (n q) -> p n q", n=DS),
                    ALU.mult), deps=dub.rd() + bb.rd() + xbb.wr())
                dub.addr(t_X)
                bb.addr(t_X)
                xbb.setw(t_X)
                hc = B("hcar")
                crb = B("car16")
                t_cf1 = em.step("dve", lambda e, d=d, dAv=dAv: e.tensor_mul(
                    car16[:, :].unsqueeze(2), dAv[:, :, 0:1],
                    hcar[:, d * DS:(d + 1) * DS].unsqueeze(2)),
                    deps=datoks + hc.rd() + crb.wr())
                dab.addr(t_cf1)
                hc.addr(t_cf1)
                crb.setw(t_cf1)
                t_cf2 = em.step("dve", lambda e: e.tensor_add(
                    Xv[:, :, 0:1], Xv[:, :, 0:1], car16[:, :].unsqueeze(2)),
                    deps=[t_X, t_cf1])
                crb.addr(t_cf2)
                xbb.setw(t_cf2)
                t_cf3 = em.step("dve", lambda e, dAv=dAv: e.memset(
                    dAv[:, :, 0:1], 0.0), deps=[t_cf1])
                dab.setw(t_cf3)
                hbb = B("Hbig")
                t_s = em.step("dve", lambda e, dA_big=dA_big: e.tensor_tensor_scan(
                    H_big[:, :], dA_big[:, :], X_big[:, :], 0.0,
                    ALU.mult, ALU.add),
                    deps=datoks + [t_cf2, t_cf3] + hbb.wr())
                dab.addr(t_s)
                xbb.addr(t_s)
                hbb.setw(t_s)
                hc = B("hcar")
                t_yh = em.step("dve", lambda e: e.tensor_tensor(
                    X_big[:, :], H_big[:, :], cbig[:, :], ALU.mult),
                    deps=[t_s] + xbb.wr() + cc.rd())
                cc.addr(t_yh)
                t_hc = em.step("dve", lambda e, d=d: e.tensor_copy(
                    hcar[:, d * DS:(d + 1) * DS].unsqueeze(2),
                    Hv[:, :, Q1 - 1:Q1]), deps=[t_s] + hc.wr())
                hc.setw(t_hc)
                hbb.addr(t_hc)
                hbb.addr(t_yh)
                xbb.setw(t_yh)
                yb = B("ysc")
                h2 = 8 * Q1
                t_t1 = em.step("dve", lambda e, h2=h2: e.tensor_add(
                    tree[:, :], X_big[:, 0:h2], X_big[:, h2:2 * h2]),
                    deps=[t_yh] + hbb.wr())
                xbb.addr(t_t1)
                hbb.setw(t_t1)
                t_t2 = em.step("dve", lambda e: e.tensor_add(
                    tree[:, 0:4 * Q1], tree[:, 0:4 * Q1], tree[:, 4 * Q1:8 * Q1]),
                    deps=[t_t1])
                hbb.setw(t_t2)
                t_t3 = em.step("dve", lambda e: e.tensor_add(
                    tree[:, 0:2 * Q1], tree[:, 0:2 * Q1], tree[:, 2 * Q1:4 * Q1]),
                    deps=[t_t2])
                hbb.setw(t_t3)
                t_rd = em.step("dve", lambda e: e.tensor_add(
                    y_sc[:, :], tree[:, 0:Q1], tree[:, Q1:2 * Q1]),
                    deps=[t_t3] + yb.wr())
                hbb.addr(t_rd)
                yb.setw(t_rd)
                t_y2a = em.step("dve", lambda e, d=d: e.scalar_tensor_tensor(
                    y_sc[:, :], uc_b[:, d * Q1:(d + 1) * Q1], dp_c[:, d:d + 1],
                    y_sc[:, :], ALU.mult, ALU.add), deps=yb.wr() + ucb.rd())
                ucb.addr(t_y2a)
                yb.setw(t_y2a)
                zb = B(f"zs{d}")
                t_y2 = em.step("pool", lambda e, d=d: e.tensor_mul(
                    uc_b[:, d * Q1:(d + 1) * Q1], y_sc[:, :],
                    zs_b[:, d * Q1:(d + 1) * Q1]),
                    deps=[t_y2a] + zb.rd() + ucb.wr())
                yb.addr(t_y2)
                zb.addr(t_y2)
                ucb.setw(t_y2)

            ftoks = {0: front(0), 1: front(1)}
            emit_bc()
            for d in range(NDT):
                back(d, ftoks.pop(d))
                if d + 2 < NDT:
                    ftoks[d + 2] = front(d + 2)


        def tail(c, x_state):
            t0 = c * Q1
            x_pair, x_sb, xb = x_state
            y2toks = sum([B(f"uc{k}").rd() for k in range(NDT)], [])
            res_toks = []
            for tt in range(TT1):
                if tt % 2 == 0:
                    psl = [(p_out[:, 0:512], "pout"), (p_out[:, 512:1024], "pout")]
                else:
                    psl = [(p_dbc[:, :], "pdbc"), (p_dt[:, :], "pdt")]
                pbs = [B(nm) for _, nm in psl]

                def opfn(e, tt=tt, psl=psl):
                    last = None
                    for k in range(NDT):
                        for h in range(2):
                            last = e.matmul(
                                psl[h][0],
                                uc_b[:, k * Q1 + tt * 128:k * Q1 + (tt + 1) * 128],
                                wout[k][:, h * 512:(h + 1) * 512],
                                start=(k == 0), stop=(k == NDT - 1))
                    return last
                t_op = em.step("pe", opfn, deps=y2toks +
                               sum([p.wr() for p in pbs], []))
                for k in range(NDT):
                    B(f"uc{k}").addr(t_op)
                for p in pbs:
                    p.setw(t_op)
                for h in range(2):
                    t_res = em.step("dve", lambda e, tt=tt, h=h, psl=psl,
                                    x_sb=x_sb: e.tensor_add(
                                        x_sb[tt][:, h * 512:(h + 1) * 512],
                                        x_sb[tt][:, h * 512:(h + 1) * 512],
                                        psl[h][0]),
                                    deps=xb.rd() + [t_op])
                    pbs[h].addr(t_res)
                    xb.addr(t_res)
                    res_toks.append(t_res)
            t_st = em.dma("pool", "st",
                          xmid_d.rearrange("b (a p) c -> b p a c", p=128)[
                              b, :, c * TT1:(c + 1) * TT1, :],
                          x_pair[:, :].rearrange("p (a c) -> p a c", a=TT1),
                          deps=res_toks)
            xb.addr(t_st)
            xmid_tok[b][c] = t_st

        xs = {0: head_a(0)}
        bcs = {0: head_b(0, xs[0])}
        for c in range(NCH1):
            if c + 1 < NCH1:
                xs[c + 1] = head_a(c + 1)
            dtile_loop(c, xs[c], bcs.pop(c))
            tail(c, xs.pop(c))
            if c + 1 < NCH1:
                bcs[c + 1] = head_b(c + 1, xs[c + 1])
            barrier["v"] = []

    p2bar = {"v": em.join()}
    ar.reset(0)
    gu_b = [ar.take(HID, BF16) for _ in range(TT2)]
    xm_pairs = [ar.take(TT2 * D, F32) for _ in range(2)]
    x2_sc = ar.take(D, F32)
    xq_bf = [ar.take(D, BF16) for _ in range(TT2)]
    xqTs = [ar.take((D // 128) * Q2, BF16) for _ in range(2)]
    xq2_i = ar.take(HID, I32)
    xq2_bf = ar.take(HID, BF16)
    xq2T = ar.take((HID // 128) * Q2, BF16)
    wg_sb = [ar.take(8 * 512, BF16) for _ in range(2)]
    wu_sb = [ar.take(8 * 512, BF16) for _ in range(2)]
    wd_sb = [ar.take(512, BF16) for _ in range(8)]
    gate_s = [ar.take(512, BF16) for _ in range(TT2)]
    assert ar.off <= ar.cols, ar.off

    def p2b():
        return p2bar["v"]

    def p2_front(b, c):
        par = c % 2
        xm_pair = xm_pairs[par]
        xm_sb = [xm_pair[:, tt * D:(tt + 1) * D] for tt in range(TT2)]
        xqT = xqTs[par]
        xmb = B(f"p2xch{par}")
        t_xl = em.dma("sp", "ld", xm_pair[:, :].rearrange(
            "p (a c) -> p a c", a=TT2),
            xmid_d.rearrange("b (a p) c -> b p a c", p=128)[
                b, :, c * TT2:(c + 1) * TT2, :],
            deps=xmb.wr() + [xmid_tok[b][c]] + p2b())
        xmb.setw(t_xl)
        xtb2 = B(f"xqT{par}")
        for tt in range(TT2):
            st8 = stat[:, tt * 8:tt * 8 + 8]
            t_s1 = em.step("dve", lambda e, tt=tt, st8=st8, xm_sb=xm_sb:
                           e.reduce_sum(st8[:, 0:1], xm_sb[tt][:, :], AX.X),
                           deps=xmb.rd() + p2b())
            xmb.addr(t_s1)
            sqb = B("p2sq")
            t_s2 = em.step("act", lambda e, tt=tt, st8=st8, xm_sb=xm_sb:
                           e.activation(
                               x2_sc[:, :], xm_sb[tt][:, :], AF.Square, bias=0.0,
                               scale=1.0, accum_out=st8[:, 1:2]),
                           deps=xmb.rd() + sqb.wr() + [t_s1] + p2b())
            xmb.addr(t_s2)
            sqb.setw(t_s2)
            t_m = em.step("dve", lambda e, st8=st8: e.tensor_scalar_mul(
                st8[:, 2:3], st8[:, 0:1], 1.0 / D), deps=[t_s1, t_s2])
            t_v1 = em.step("dve", lambda e, st8=st8: e.tensor_scalar_mul(
                st8[:, 3:4], st8[:, 1:2], 1.0 / D), deps=[t_s2])
            t_v2 = em.step("dve", lambda e, st8=st8: e.tensor_mul(
                st8[:, 4:5], st8[:, 2:3], st8[:, 2:3]), deps=[t_m])
            t_v3 = em.step("dve", lambda e, st8=st8: e.tensor_sub(
                st8[:, 5:6], st8[:, 3:4], st8[:, 4:5]), deps=[t_v1, t_v2])
            t_sd = em.step("act", lambda e, st8=st8: e.activation(
                st8[:, 6:7], st8[:, 5:6], AF.Sqrt, bias=ccols[:, 0:1], scale=1.0),
                deps=[t_v3])
            t_rst = em.step("dve", lambda e, st8=st8: e.reciprocal(
                st8[:, 7:8], st8[:, 6:7]), deps=[t_sd])
            x2b = B("p2x2sc")
            t_x2 = em.step("dve", lambda e, tt=tt, st8=st8, xm_sb=xm_sb:
                           e.tensor_scalar(
                               x2_sc[:, :], xm_sb[tt][:, :], st8[:, 2:3],
                               st8[:, 7:8], ALU.subtract, ALU.mult),
                           deps=xmb.rd() + [t_m, t_rst, t_s2] + x2b.wr() + sqb.wr())
            xmb.addr(t_x2)
            x2b.setw(t_x2)
            sqb.setw(t_x2)
            sc8 = statb[:, par * 16 + tt * 4:par * 16 + tt * 4 + 4]
            scb = B(f"sc{par}{tt}")
            t_am = em.step("dve", lambda e, sc8=sc8: e.reduce_max(
                sc8[:, 0:1], x2_sc[:, :], AX.X, apply_absolute_value=True),
                deps=[t_x2] + scb.wr())
            x2b.addr(t_am)
            scb.setw(t_am)
            t_si = em.step("dve", lambda e, sc8=sc8: e.tensor_scalar(
                sc8[:, 1:2], sc8[:, 0:1], 1e-5, 1.0 / 127.0, ALU.max, ALU.mult),
                deps=[t_am])
            scb.setw(t_si)
            t_sq = em.step("dve", lambda e, sc8=sc8: e.reciprocal(
                sc8[:, 2:3], sc8[:, 1:2]), deps=[t_si])
            scb.setw(t_sq)
            t_sg = em.step("dve", lambda e, sc8=sc8: e.tensor_mul(
                sc8[:, 3:4], sc8[:, 1:2], cm_c[:, 0:1]),
                deps=[t_si] + B("cm").rd())
            B("cm").addr(t_sg)
            scb.setw(t_sg)
            t_su = em.step("dve", lambda e, sc8=sc8, tt=tt, par=par: e.tensor_mul(
                stat[:, 32 + par * 4 + tt:33 + par * 4 + tt], sc8[:, 1:2],
                cm_c[:, 1:2]), deps=[t_si] + B("cm").rd())
            B("cm").addr(t_su)
            scb.setw(t_su)
            qib = B("p2qi")
            t_qi = em.step("act", lambda e, sc8=sc8: e.activation(
                xq2_i[:, :D], x2_sc[:, :], AF.Copy, bias=0.0,
                scale=sc8[:, 2:3]), deps=[t_x2, t_sq] + qib.wr())
            x2b.addr(t_qi)
            scb.addr(t_qi)
            qib.setw(t_qi)
            qbb = B(f"xqbf{tt}")
            t_qb = em.step("dve", lambda e, tt=tt: e.tensor_copy(
                xq_bf[tt][:, :], xq2_i[:, :D]), deps=[t_qi] + qbb.wr())
            qib.addr(t_qb)
            qbb.setw(t_qb)
            for g4 in range(2):
                ptr_t, ptr_n = tr_bank()
                ptb = B(ptr_n)
                tts = []
                for j in range(4):
                    db = g4 * 4 + j
                    t_t = em.step("pe", lambda e, db=db, j=j, tt=tt,
                                  ptr_t=ptr_t: e.transpose(
                        ptr_t[:, j * 128:(j + 1) * 128],
                        xq_bf[tt][:, db * 128:(db + 1) * 128], eye[:, :]),
                        deps=qbb.rd() + (ptb.wr() if j == 0 else []) + p2b())
                    qbb.addr(t_t)
                    if j == 0:
                        ptb.setw(t_t)
                    tts.append(t_t)
                dstv = xqT.rearrange("p (db q) -> p db q", db=D // 128)[
                    :, g4 * 4:(g4 + 1) * 4, tt * 128:(tt + 1) * 128]
                t_ev = em.step("act", lambda e, dstv=dstv, ptr_t=ptr_t: e.copy(
                    dstv, ptr_t[:, :].rearrange("p (a c) -> p a c", a=4)),
                    deps=tts + (xtb2.wr() if (tt == 0 and g4 == 0) else []))
                ptb.addr(t_ev)
                if tt == 0 and g4 == 0:
                    xtb2.setw(t_ev)
                else:
                    xtb2.addr(t_ev)
        p2bar["v"] = []

    def p2_mid(b, c):
        par = c % 2
        xqT = xqTs[par]
        xtb2 = B(f"xqT{par}")
        xqT_tok = [xtb2.w] + list(xtb2.readers)
        def quant2(tt):
            gub = B(f"gu{tt}")
            gutoks = gub.rd() + list(gub.readers)
            sc8 = statb[:, 32 + tt * 4:32 + tt * 4 + 4]
            scb = B(f"sc2{tt}")
            t_am = em.step("dve", lambda e, tt=tt, sc8=sc8: e.reduce_max(
                sc8[:, 0:1], gu_b[tt][:, :], AX.X, apply_absolute_value=True),
                deps=gutoks + scb.wr())
            gub.addr(t_am)
            scb.setw(t_am)
            t_si = em.step("dve", lambda e, sc8=sc8: e.tensor_scalar(
                sc8[:, 1:2], sc8[:, 0:1], 1e-5, 1.0 / 127.0, ALU.max, ALU.mult),
                deps=[t_am])
            scb.setw(t_si)
            t_sq = em.step("dve", lambda e, sc8=sc8: e.reciprocal(
                sc8[:, 2:3], sc8[:, 1:2]), deps=[t_si])
            scb.setw(t_sq)
            t_sd2 = em.step("dve", lambda e, sc8=sc8: e.tensor_mul(
                sc8[:, 3:4], sc8[:, 1:2], cm_c[:, 2:3]),
                deps=[t_si] + B("cm").rd())
            B("cm").addr(t_sd2)
            scb.setw(t_sd2)
            qib = B("p2qi")
            t_qi = em.step("act", lambda e, tt=tt, sc8=sc8: e.activation(
                xq2_i[:, :], gu_b[tt][:, :], AF.Copy, bias=0.0,
                scale=sc8[:, 2:3]), deps=gutoks + [t_sq] + qib.wr())
            gub.addr(t_qi)
            scb.addr(t_qi)
            qib.setw(t_qi)
            qbb = B("xq2bf")
            t_qb = em.step("dve", lambda e: e.tensor_copy(
                xq2_bf[:, :], xq2_i[:, :]), deps=[t_qi] + qbb.wr())
            qib.addr(t_qb)
            qbb.setw(t_qb)
            xtb3 = B("xq2T")
            for g8 in range(8):
                ptr_t, ptr_n = tr_bank()
                ptb = B(ptr_n)
                tts = []
                for j in range(4):
                    ib = g8 * 4 + j
                    t_t = em.step("pe", lambda e, ib=ib, j=j,
                                  ptr_t=ptr_t: e.transpose(
                        ptr_t[:, j * 128:(j + 1) * 128],
                        xq2_bf[:, ib * 128:(ib + 1) * 128], eye[:, :]),
                        deps=qbb.rd() + (ptb.wr() if j == 0 else []))
                    qbb.addr(t_t)
                    if j == 0:
                        ptb.setw(t_t)
                    tts.append(t_t)
                dstv = xq2T.rearrange("p (ib q) -> p ib q", ib=HID // 128)[
                    :, g8 * 4:(g8 + 1) * 4, tt * 128:(tt + 1) * 128]
                t_ev = em.step("act", lambda e, dstv=dstv, ptr_t=ptr_t: e.copy(
                    dstv, ptr_t[:, :].rearrange("p (a c) -> p a c", a=4)),
                    deps=tts + (xtb3.wr() if (tt == 0 and g8 == 0) else []))
                ptb.addr(t_ev)
                if tt == 0 and g8 == 0:
                    xtb3.setw(t_ev)
                else:
                    xtb3.addr(t_ev)

        NS8 = HID // 512
        for s in range(NS8):
            wgb = B(f"wg{s % 2}")
            wub = B(f"wu{s % 2}")
            tg = []
            tu = []
            for k in range(8):
                tg.append(em.dma("sp", "ld", wg_sb[s % 2][:, k * 512:(k + 1) * 512],
                                 wqt_d["g"][k * 128:(k + 1) * 128,
                                            s * 512:(s + 1) * 512],
                                 deps=(wgb.wr() if k == 0 else [])))
                tu.append(em.dma("sp", "ld", wu_sb[s % 2][:, k * 512:(k + 1) * 512],
                                 wqt_d["u"][k * 128:(k + 1) * 128,
                                            s * 512:(s + 1) * 512],
                                 deps=(wub.wr() if k == 0 else [])))
            wgb.setw(tg[-1])
            for t in tg[:-1]:
                wgb.addr(t)
            wub.setw(tu[-1])
            for t in tu[:-1]:
                wub.addr(t)
            for tt in range(TT2):
                sc8 = statb[:, par * 16 + tt * 4:par * 16 + tt * 4 + 4]
                pg_t = (p_mm0, p_dbc)[tt % 2]
                pg = B(("pmm0", "pdbc")[tt % 2])
                t_g = em.step("pe", lambda e, s=s, tt=tt, pg_t=pg_t, xqT=xqT: _gmm(
                    e, pg_t, xqT, wg_sb[s % 2], tt),
                    deps=xqT_tok + tg + pg.wr())
                xtb2.addr(t_g)
                wgb.addr(t_g)
                pg.setw(t_g)
                gsb = B(f"gs{tt}")
                t_sil = em.step("act", lambda e, tt=tt, sc8=sc8,
                                pg_t=pg_t: e.activation(
                    gate_s[tt][:, :], pg_t[:, :], AF.Silu, bias=0.0,
                    scale=sc8[:, 3:4]), deps=pg.rd() + gsb.wr() +
                    B(f"sc{par}{tt}").rd())
                pg.addr(t_sil)
                B(f"sc{par}{tt}").addr(t_sil)
                gsb.setw(t_sil)
                pu_t = (p_mm1, p_dt)[tt % 2]
                pu = B(("pmm1", "pdt")[tt % 2])
                t_u = em.step("pe", lambda e, s=s, tt=tt, pu_t=pu_t, xqT=xqT: _gmm(
                    e, pu_t, xqT, wu_sb[s % 2], tt),
                    deps=xqT_tok + tu + pu.wr())
                xtb2.addr(t_u)
                wub.addr(t_u)
                pu.setw(t_u)
                gub = B(f"gu{tt}")
                t_gu = em.step("dve", lambda e, s=s, tt=tt, pu_t=pu_t,
                               par=par: e.scalar_tensor_tensor(
                    gu_b[tt][:, s * 512:(s + 1) * 512], pu_t[:, :],
                    stat[:, 32 + par * 4 + tt:33 + par * 4 + tt],
                    gate_s[tt][:, :], ALU.mult, ALU.mult),
                    deps=pu.rd() + [t_sil] + B(f"sc{par}{tt}").rd() +
                    (gub.wr() if s == 0 else []))
                pu.addr(t_gu)
                gsb.addr(t_gu)
                B(f"sc{par}{tt}").addr(t_gu)
                if s == 0:
                    gub.setw(t_gu)
                else:
                    gub.addr(t_gu)
                if s == NS8 - 1:
                    quant2(tt)

    def p2_down(b, c):
        par = c % 2
        t0 = c * Q2
        xm_pair = xm_pairs[par]
        xm_sb = [xm_pair[:, tt * D:(tt + 1) * D] for tt in range(TT2)]
        xmb = B(f"p2xch{par}")
        xtb3 = B("xq2T")
        xq2T_tok = [xtb3.w] + list(xtb3.readers)
        psums = (p_mm0, p_mm1, p_dbc, p_dt)
        pnames = ("pmm0", "pmm1", "pdbc", "pdt")
        for dmh in range(2):
            for k in range(HID // 128):
                wdb = B(f"wd{k % 8}")
                t_w = em.dma("sp", "ld", wd_sb[k % 8][:, :],
                             wqt_d["d"][k * 128:(k + 1) * 128,
                                        dmh * 512:(dmh + 1) * 512],
                             deps=wdb.wr())
                wdb.setw(t_w)

                def dnfn(e, k=k, dmh=dmh):
                    last = None
                    for tt in range(TT2):
                        last = e.matmul(
                            psums[tt][:, :],
                            xq2T[:, k * Q2 + tt * 128:k * Q2 + (tt + 1) * 128],
                            wd_sb[k % 8][:, :],
                            start=(k == 0), stop=(k == HID // 128 - 1))
                    return last
                pd = [B(p) for p in pnames]
                t_dn = em.step("pe", dnfn,
                               deps=xq2T_tok + wdb.rd() +
                               (sum([p.wr() for p in pd], []) if k == 0 else []))
                wdb.addr(t_dn)
                xtb3.addr(t_dn)
                if k == 0:
                    for p in pd:
                        p.setw(t_dn)
                else:
                    for p in pd:
                        p.addr(t_dn)
            for tt in range(TT2):
                sc8 = statb[:, 32 + tt * 4:32 + tt * 4 + 4]
                pb_ = B(pnames[tt])
                xb = B(f"p2x{par}{tt}")
                t_r = em.step("dve", lambda e, tt=tt, dmh=dmh, sc8=sc8,
                              xm_sb=xm_sb: e.scalar_tensor_tensor(
                                  xm_sb[tt][:, dmh * 512:(dmh + 1) * 512],
                                  psums[tt][:, :], sc8[:, 3:4],
                                  xm_sb[tt][:, dmh * 512:(dmh + 1) * 512],
                                  ALU.mult, ALU.add),
                              deps=pb_.wr() + xmb.rd() + B(f"sc2{tt}").rd() +
                              xb.wr())
                pb_.addr(t_r)
                B(f"sc2{tt}").addr(t_r)
                xmb.addr(t_r)
                xb.setw(t_r)
        st_deps = sum([B(f"p2x{par}{t_}").rd() for t_ in range(TT2)], [])
        t_st = em.dma("pool", "st",
                      out_d.rearrange("b (a p) c -> b p a c", p=128)[
                          b, :, c * TT2:(c + 1) * TT2, :],
                      xm_pair[:, :].rearrange("p (a c) -> p a c", a=TT2),
                      deps=st_deps)
        for t_ in range(TT2):
            B(f"p2x{par}{t_}").addr(t_st)
        xmb.addr(t_st)

    for b in range(NB):
        p2_front(b, 0)
        for c in range(NCH2):
            p2_mid(b, c)
            if c + 1 < NCH2:
                p2_front(b, c + 1)
            p2_down(b, c)

    em.step("pool", lambda e: e.nop(), deps=[("st", em.dcnt["st"]),
                                             ("ld", em.dcnt["ld"])] + em.join())

    import contextlib
    with contextlib.ExitStack() as es:
        sems = {}
        for s in ENGS:
            sems[s] = es.enter_context(nc.semaphore(f"sem_{s}"))
        for s in DMAS:
            sems[s] = es.enter_context(nc.semaphore(f"sem_{s}"))
        with nc.Block() as block:
            em.emit(block, sems)
    return nc


def _gmm(e, psum, xqT, w_sb, tt):
    last = None
    for k in range(8):
        last = e.matmul(psum[:, :], xqT[:, k * Q2 + tt * 128:k * Q2 + (tt + 1) * 128],
                        w_sb[:, k * 512:(k + 1) * 512],
                        start=(k == 0), stop=(k == 7))
    return last


_CONSTS = None


def _consts():
    global _CONSTS
    if _CONSTS is None:
        sel = np.zeros((32, 32 * 128), dtype=ml_dtypes.bfloat16)
        for n in range(32):
            sel[n, n * 128:(n + 1) * 128] = 1.0
        _CONSTS = {
            "c_eye": np.eye(128, dtype=ml_dtypes.bfloat16),
            "c_sel": sel,
            "c_onesrow": np.ones((1, 128), dtype=np.float32),
            "c_onesb": np.ones((128, 1), dtype=ml_dtypes.bfloat16),
            "c_cols": np.concatenate([np.full((128, 1), 1e-5, np.float32),
                                      np.ones((128, 1), np.float32)], axis=1),
        }
    return _CONSTS


def kernel(**inputs):
    x = np.ascontiguousarray(inputs["L0_attn_x"])
    weights = {k: np.ascontiguousarray(v) for k, v in inputs.items()
               if k != "L0_attn_x"}
    nc = build_program()
    consts = _consts()
    in_maps = []
    for c in range(8):
        m = {"L0_attn_x": x[2 * c:2 * c + 2]}
        m.update(weights)
        m.update(consts)
        in_maps.append(m)
    res = run_bass_kernel_spmd(nc, in_maps, core_ids=list(range(8)))
    return np.concatenate([r["out"] for r in res.results], axis=0)


# version 38 (final)
# speedup vs baseline: 1.0381x; 1.0381x over previous
import numpy as np
import ml_dtypes
import concourse.bass as bass
import concourse.mybir as mybir
from concourse.bass_utils import run_bass_kernel_spmd

F32 = mybir.dt.float32
BF16 = mybir.dt.bfloat16
I32 = mybir.dt.int32
AF = mybir.ActivationFunctionType
ALU = mybir.AluOpType
AX = mybir.AxisListType

NB = 2
T = 4096
D = 1024
DI = 2048
NDT = DI // 128
DS = 16
RK = 64
DC = 4
HID = 4096
EPS = 1e-5

Q1 = 256
TT1 = Q1 // 128
NCH1 = T // Q1
NG = 2
Q2 = 512
TT2 = Q2 // 128
NCH2 = T // Q2

ENGS = ("pe", "act", "dve", "pool", "sp")
DMAS = ("ld", "st")
_DBG = {"on": False}


class Buf:
    __slots__ = ("w", "readers")

    def __init__(self):
        self.w = None
        self.readers = []

    def rd(self):
        return [self.w] if self.w is not None else []

    def wr(self):
        d = [self.w] if self.w is not None else []
        return d + self.readers

    def setw(self, tok):
        self.w = tok
        self.readers = []

    def addr(self, tok):
        self.readers.append(tok)


class Em:
    def __init__(self):
        self.q = {e: [] for e in ENGS}
        self.cnt = {e: 0 for e in ENGS}
        self.dcnt = {k: 0 for k in DMAS}

    def step(self, eng, fn, deps=()):
        deps = [t for t in deps if t is not None]
        self.cnt[eng] += 1
        tok = (eng, self.cnt[eng])
        self.q[eng].append(("c", eng, deps, fn))
        return tok

    def dma(self, queue, sem, dst, src, deps=()):
        deps = [t for t in deps if t is not None]
        self.dcnt[sem] += 16
        tok = (sem, self.dcnt[sem])
        self.q[queue].append(("d", sem, deps, (dst, src)))
        return tok

    def join(self):
        t = [(e, self.cnt[e]) for e in ENGS if self.cnt[e] > 0]
        t += [(k, self.dcnt[k]) for k in DMAS if self.dcnt[k] > 0]
        return t

    def emit(self, block, sems):
        def mk(eng_name):
            def body(eng):
                wm = {}
                for kind, key, deps, payload in self.q[eng_name]:
                    best = {}
                    for (sk, sv) in deps:
                        if wm.get(sk, 0) < sv and best.get(sk, 0) < sv:
                            best[sk] = sv
                    for sk in sorted(best):
                        eng.wait_ge(sems[sk], best[sk])
                        wm[sk] = best[sk]
                    if kind == "c":
                        payload(eng).then_inc(sems[key], 1)
                    else:
                        dst, src = payload
                        eng.dma_start(dst, src).then_inc(sems[key], 16)
            return body
        block.tensor(mk("pe"))
        block.scalar(mk("act"))
        block.vector(mk("dve"))
        block.gpsimd(mk("pool"))
        block.sync(mk("sp"))


class Arena:
    def __init__(self, nc, name, cols):
        self.t = nc.alloc_sbuf_tensor(name, [128, cols], BF16)
        self.cols = cols
        self.off = 0

    def take(self, n_elem, dt):
        unit = {F32: 2, BF16: 1, I32: 2}[dt]
        self.off = (self.off + 1) & ~1
        o = self.off
        self.off += n_elem * unit
        assert self.off <= self.cols, f"arena overflow {self.off} > {self.cols}"
        ap = self.t[:, o:o + n_elem * unit]
        if dt != BF16:
            ap = ap.bitcast(dt)
        return ap

    def mark(self):
        return self.off

    def reset(self, off=0):
        self.off = off


def build_program():
    nc = bass.Bass("TRN2", target_bir_lowering=False)

    x_in = nc.dram_tensor("L0_attn_x", [NB, T, D], F32, kind="ExternalInput")
    w_inproj = nc.dram_tensor("L0_attn_in_proj_w", [D, 2 * DI], F32, kind="ExternalInput")
    w_conv = nc.dram_tensor("L0_attn_conv_w", [DI, 1, DC], F32, kind="ExternalInput")
    b_conv = nc.dram_tensor("L0_attn_conv_b", [DI], F32, kind="ExternalInput")
    w_xproj = nc.dram_tensor("L0_attn_x_proj_w", [DI, RK + 2 * DS], F32, kind="ExternalInput")
    w_dtp = nc.dram_tensor("L0_attn_dt_proj_w", [RK, DI], F32, kind="ExternalInput")
    b_dt = nc.dram_tensor("L0_attn_dt_proj_b", [DI], F32, kind="ExternalInput")
    a_log = nc.dram_tensor("L0_attn_A_log", [DI, DS], F32, kind="ExternalInput")
    dp_in = nc.dram_tensor("L0_attn_Dp", [DI], F32, kind="ExternalInput")
    w_out = nc.dram_tensor("L0_attn_out_proj_w", [DI, D], F32, kind="ExternalInput")
    w_ffn = {
        "g": nc.dram_tensor("L0_ffn_w_gate", [HID, D], F32, kind="ExternalInput"),
        "u": nc.dram_tensor("L0_ffn_w_up", [HID, D], F32, kind="ExternalInput"),
        "d": nc.dram_tensor("L0_ffn_w_down", [D, HID], F32, kind="ExternalInput"),
    }
    c_eye = nc.dram_tensor("c_eye", [128, 128], BF16, kind="ExternalInput")
    c_sel = nc.dram_tensor("c_sel", [32, 32 * 128], BF16, kind="ExternalInput")
    c_onesrow = nc.dram_tensor("c_onesrow", [1, 128], F32, kind="ExternalInput")
    c_onesb = nc.dram_tensor("c_onesb", [128, 1], BF16, kind="ExternalInput")
    c_cols = nc.dram_tensor("c_cols", [128, 2], F32, kind="ExternalInput")

    out_d = nc.dram_tensor("out", [NB, T, D], F32, kind="ExternalOutput")
    if _DBG["on"]:
        dbg_xq = nc.dram_tensor("dbg_xq", [TT2, 128, D], BF16, kind="ExternalOutput")
        dbg_gu = nc.dram_tensor("dbg_gu", [TT2, 128, HID], F32, kind="ExternalOutput")
        dbg_sc = nc.dram_tensor("dbg_sc", [2, 128, 16], F32, kind="ExternalOutput")
    xmid_d = nc.dram_tensor("xmid", [NB, T, D], F32, kind="Internal")
    wqt_d = {
        "g": nc.dram_tensor("wqtg", [D, HID], BF16, kind="Internal"),
        "u": nc.dram_tensor("wqtu", [D, HID], BF16, kind="Internal"),
        "d": nc.dram_tensor("wqtd", [HID, D], BF16, kind="Internal"),
    }

    em = Em()
    bufs = {}

    def B(name):
        if name not in bufs:
            bufs[name] = Buf()
        return bufs[name]

    eye = nc.alloc_sbuf_tensor("eye", [128, 128], BF16)
    onesrow = nc.alloc_sbuf_tensor("onesrow", [1, 128], F32)
    onesb = nc.alloc_sbuf_tensor("onesb", [128, 1], BF16)
    cm_c = nc.alloc_sbuf_tensor("cm_c", [128, 4], F32)
    stat = nc.alloc_sbuf_tensor("stat", [128, 64], F32)
    statb = nc.alloc_sbuf_tensor("statb", [128, 48], F32)
    ccols = nc.alloc_sbuf_tensor("ccols", [128, 2], F32)

    ar = Arena(nc, "arena", 105700)

    p_tr = nc.alloc_psum_tensor("p_tr", [128, 512], BF16)
    p_tr2 = nc.alloc_psum_tensor("p_tr2", [128, 512], BF16)
    trrot = {"i": 0}

    def tr_bank():
        i = trrot["i"] % 2
        trrot["i"] += 1
        return (p_tr, p_tr2)[i], f"ptr{i}"
    p_mm0 = nc.alloc_psum_tensor("p_mm0", [128, 512], F32)
    p_mm1 = nc.alloc_psum_tensor("p_mm1", [128, 512], F32)
    p_dbc = nc.alloc_psum_tensor("p_dbc", [128, 512], F32)
    p_dt = nc.alloc_psum_tensor("p_dt", [128, 512], F32)
    p_out = nc.alloc_psum_tensor("p_out", [128, 1024], F32)

    win = [ar.take(2 * DI, BF16) for _ in range(D // 128)]
    wout = [ar.take(D, BF16) for _ in range(NDT)]
    wxp = [ar.take(RK + 2 * DS, BF16) for _ in range(NDT)]
    sel = ar.take(32 * 128, BF16)
    wdt = sel[:, 0:DI]
    a_cols = ar.take(NDT * DS, F32)
    convw = ar.take(NDT * DC, F32)
    convb_c = ar.take(NDT, F32)
    dtb_c = ar.take(NDT, F32)
    dp_c = ar.take(NDT, F32)
    hcar = ar.take(NDT * DS, F32)
    uhalo = ar.take(NDT * (DC - 1), BF16)
    p1_cb = ar.mark()

    xln_st = [ar.take(D, BF16)]
    bbig = ar.take(DS * Q1, BF16)
    cbig = ar.take(DS * Q1, BF16)
    xlnT = ar.take((D // 128) * Q1, BF16)
    x_pairs = [ar.take(TT1 * D, F32) for _ in range(2)]
    u_sc = [ar.take(DC - 1 + Q1 + 1, BF16) for _ in range(2)]
    uc_b = ar.take(NDT * Q1, BF16)
    zs_b = ar.take(NDT * Q1, BF16)
    dbc_b = ar.take(Q1, BF16)
    dA_bufs = [ar.take(DS * Q1, BF16) for _ in range(2)]
    X_big = ar.take(DS * Q1, BF16)
    H_big = ar.take(DS * Q1, BF16)
    tree = H_big.bitcast(F32)
    dt_sc = ar.take(Q1, F32)
    a_sc = ar.take(Q1, F32)
    dtu_bufs = [ar.take(Q1, BF16) for _ in range(2)]
    y_sc = ar.take(Q1, F32)
    cacc = ar.take(Q1, F32)
    cacc2 = ar.take(Q1, F32)
    car16 = ar.take(DS, F32)
    p1_end = ar.mark()

    ar.reset(p1_cb)
    stage = [ar.take(4096, F32) for _ in range(2)]
    wq_i = ar.take(HID, I32)
    wq_b = ar.take(HID, BF16)
    abs_b = ar.take(HID, BF16)
    wqt_s = ar.take(512, BF16)
    ar.reset(p1_end)

    tok_consts = [
        em.dma("sp", "ld", eye[:, :], c_eye[:, :]),
        em.dma("sp", "ld", sel[64:96, :], c_sel[:, :]),
        em.dma("sp", "ld", onesrow[:, :], c_onesrow[:, :]),
        em.dma("sp", "ld", onesb[:, :], c_onesb[:, :]),
        em.dma("sp", "ld", ccols[:, :], c_cols[:, :]),
    ]
    prep_last = list(tok_consts)

    def load_convert(dram, rows, cols, dsts):
        for k in range(rows // 128):
            st = stage[k % 2]
            sb = B(f"stg{k % 2}")
            if cols >= 1024:
                h = cols // 2
                t_l0 = em.dma("sp", "ld", st[:, :h],
                              dram[k * 128:(k + 1) * 128, 0:h], deps=sb.wr())
                t_ld = em.dma("sp", "ld", st[:, h:cols],
                              dram[k * 128:(k + 1) * 128, h:cols], deps=[])
                sb.setw(t_ld)
                sb.addr(t_l0)
            else:
                t_ld = em.dma("sp", "ld", st[:, :cols],
                              dram[k * 128:(k + 1) * 128, :], deps=sb.wr())
                sb.setw(t_ld)
            t_cp = em.step("dve", lambda e, st=st, k=k, cols=cols, dsts=dsts:
                           e.tensor_copy(dsts[k][:, :cols], st[:, :cols]),
                           deps=sb.rd())
            sb.addr(t_cp)
            prep_last.append(t_cp)

    load_convert(w_inproj, D, 2 * DI, win)
    load_convert(w_out, DI, D, wout)
    load_convert(w_xproj, DI, RK + 2 * DS, wxp)

    sb = B("stg0")
    t_ld = em.dma("sp", "ld", stage[0][0:RK, :DI], w_dtp[:, :], deps=sb.wr())
    sb.setw(t_ld)
    t_cp = em.step("dve", lambda e: e.tensor_copy(wdt[0:RK, :], stage[0][0:RK, :DI]),
                   deps=sb.rd())
    sb.addr(t_cp)
    prep_last.append(t_cp)

    for k in range(NDT):
        st = stage[k % 2]
        sb = B(f"stg{k % 2}")
        t_ld = em.dma("sp", "ld", st[:, :DS], a_log[k * 128:(k + 1) * 128, :],
                      deps=sb.wr())
        sb.setw(t_ld)
        t_e = em.step("act", lambda e, st=st: e.activation(
            st[:, 64:64 + DS], st[:, :DS], AF.Exp, bias=0.0, scale=1.0),
            deps=sb.rd())
        sb.addr(t_e)
        sb.setw(t_e)
        t_n = em.step("dve", lambda e, st=st, k=k: e.tensor_scalar_mul(
            a_cols[:, k * DS:(k + 1) * DS], st[:, 64:64 + DS], -1.0), deps=[t_e])
        sb.addr(t_n)
        prep_last.append(t_n)

    wc2 = w_conv.rearrange("d one c -> d (one c)")
    bc2 = b_conv.rearrange("(a b) -> a b", b=1)
    bd2 = b_dt.rearrange("(a b) -> a b", b=1)
    dp2 = dp_in.rearrange("(a b) -> a b", b=1)
    for k in range(NDT):
        prep_last.append(em.dma("sp", "ld", convw[:, k * DC:(k + 1) * DC],
                                wc2[k * 128:(k + 1) * 128, :]))
        prep_last.append(em.dma("sp", "ld", convb_c[:, k:k + 1],
                                bc2[k * 128:(k + 1) * 128, :]))
        prep_last.append(em.dma("sp", "ld", dtb_c[:, k:k + 1],
                                bd2[k * 128:(k + 1) * 128, :]))
        prep_last.append(em.dma("sp", "ld", dp_c[:, k:k + 1],
                                dp2[k * 128:(k + 1) * 128, :]))

    def ffn_quant(wdram, rows, cols, outdram, cm_idx, tag):
        ntile = rows // 128
        pa = B("pdbc")
        ab = B("absb")
        for r in range(ntile):
            st = stage[r % 2]
            sb = B(f"stg{r % 2}")
            h = cols // 2
            t_l0 = em.dma("sp", "ld", st[:, :h],
                          wdram[r * 128:(r + 1) * 128, 0:h], deps=sb.wr())
            t_ld = em.dma("sp", "ld", st[:, h:cols],
                          wdram[r * 128:(r + 1) * 128, h:cols], deps=[])
            sb.setw(t_ld)
            sb.addr(t_l0)
            t_ab = em.step("act", lambda e, st=st, cols=cols: e.activation(
                abs_b[:, :cols], st[:, :cols], AF.Abs, bias=0.0, scale=1.0),
                deps=sb.rd() + ab.wr())
            sb.addr(t_ab)
            ab.setw(t_ab)

            nh = (cols + 511) // 512

            def mmfn(e, r=r, cols=cols, ntile=ntile, nh=nh):
                last = None
                for h in range(nh):
                    w = min(512, cols - h * 512)
                    last = e.matmul(p_dbc[0:1, 0:w], onesb[:, 0:1],
                                    abs_b[:, h * 512:h * 512 + w],
                                    start=(r == 0 and h == 0),
                                    stop=(r == ntile - 1 and h == nh - 1))
                return last
            t_mm = em.step("pe", mmfn,
                           deps=ab.rd() + tok_consts + (pa.wr() if r == 0 else []))
            ab.addr(t_mm)
            if r == 0:
                pa.setw(t_mm)
            else:
                pa.addr(t_mm)
        msb = B("statb")
        t_rs = em.step("dve", lambda e: e.reduce_sum(
            statb[0:1, 0:1], p_dbc[0:1, :512], AX.X), deps=pa.wr() + msb.wr())
        pa.addr(t_rs)
        msb.setw(t_rs)
        t_mean = em.step("dve", lambda e, rows=rows, cols=cols: e.tensor_scalar(
            statb[0:1, 1:2], statb[0:1, 0:1], 1.0 / (rows * cols), 1e-5,
            ALU.mult, ALU.max), deps=[t_rs])
        msb.addr(t_mean)
        pb = B("pdt")
        t_bc = em.step("pe", lambda e: e.matmul(
            p_dt[:, 0:1], onesrow[0:1, :], statb[0:1, 1:2], start=True, stop=True),
            deps=[t_mean] + pb.wr() + tok_consts)
        msb.addr(t_bc)
        pb.setw(t_bc)
        cmb = B("cm")
        t_cm = em.step("act", lambda e, cm_idx=cm_idx: e.copy(
            cm_c[:, cm_idx:cm_idx + 1], p_dt[:, 0:1]), deps=pb.rd() + cmb.wr())
        pb.addr(t_cm)
        cmb.setw(t_cm)
        scb = B("wscale")
        t_sc = em.step("dve", lambda e, cm_idx=cm_idx: e.reciprocal(
            statb[:, 2:3], cm_c[:, cm_idx:cm_idx + 1]),
            deps=[t_cm] + scb.wr() + [t_mean])
        cmb.addr(t_sc)
        msb.addr(t_sc)
        scb.setw(t_sc)
        for r in range(ntile):
            st = stage[r % 2]
            sb = B(f"stg{r % 2}")
            h = cols // 2
            t_l0 = em.dma("sp", "ld", st[:, :h],
                          wdram[r * 128:(r + 1) * 128, 0:h], deps=sb.wr())
            t_ld = em.dma("sp", "ld", st[:, h:cols],
                          wdram[r * 128:(r + 1) * 128, h:cols], deps=[])
            sb.setw(t_ld)
            sb.addr(t_l0)
            qi = B("wqi")
            t_q = em.step("act", lambda e, st=st, cols=cols: e.activation(
                wq_i[:, :cols], st[:, :cols], AF.Copy, bias=0.0, scale=statb[:, 2:3]),
                deps=sb.rd() + qi.wr() + scb.rd())
            sb.addr(t_q)
            scb.addr(t_q)
            qi.setw(t_q)
            qb = B("wqb")
            t_c = em.step("dve", lambda e, cols=cols: e.tensor_scalar(
                wq_b[:, :cols], wq_i[:, :cols], 1.0, -1.0, ALU.min, ALU.max),
                deps=qi.rd() + qb.wr())
            qi.addr(t_c)
            qb.setw(t_c)
            for g in range(cols // 512):
                ptr_t, ptr_n = tr_bank()
                ptb = B(ptr_n)
                tts = []
                for j in range(4):
                    db = g * 4 + j
                    t_t = em.step("pe", lambda e, db=db, j=j, ptr_t=ptr_t: e.transpose(
                        ptr_t[:, j * 128:(j + 1) * 128],
                        wq_b[:, db * 128:(db + 1) * 128], eye[:, :]),
                        deps=qb.rd() + tok_consts + (ptb.wr() if j == 0 else []))
                    qb.addr(t_t)
                    if j == 0:
                        ptb.setw(t_t)
                    tts.append(t_t)
                ws = B("wqts")
                t_e = em.step("act", lambda e, ptr_t=ptr_t: e.copy(
                    wqt_s[:, :], ptr_t[:, :]), deps=tts + ws.wr())
                ptb.addr(t_e)
                ws.setw(t_e)
                dst = outdram.rearrange("(a p) c -> p a c", p=128)[
                    :, g * 4:(g + 1) * 4, r * 128:(r + 1) * 128]
                src = wqt_s[:, :].rearrange("p (a c) -> p a c", a=4)
                t_st = em.dma("pool", "st", dst, src, deps=ws.rd())
                ws.addr(t_st)
                prep_last.append(t_st)

    ffn_quant(w_ffn["g"], HID, D, wqt_d["g"], 0, "g")
    ffn_quant(w_ffn["u"], HID, D, wqt_d["u"], 1, "u")
    ffn_quant(w_ffn["d"], D, HID, wqt_d["d"], 2, "d")

    barrier = {"v": prep_last + em.join()}

    def bar():
        return barrier["v"]

    xmid_tok = [[None] * NCH1 for _ in range(NB)]

    for b in range(NB):
        t_i1 = em.step("dve", lambda e: e.memset(hcar[:, :], 0.0),
                       deps=B("hcar").wr() + bar())
        B("hcar").setw(t_i1)
        t_i2 = em.step("dve", lambda e: e.memset(uhalo[:, :], 0.0),
                       deps=B("uhalo").wr() + bar())
        B("uhalo").setw(t_i2)
        for d in range(NDT):
            B(f"uh{d}").setw(t_i2)

        def head_a(c):
            t0 = c * Q1
            x_pair = x_pairs[c % 2]
            x_sb = [x_pair[:, tt * D:(tt + 1) * D] for tt in range(TT1)]
            xb = B(f"xch{c % 2}")
            t_x = em.dma("sp", "ld", x_pair[:, :].rearrange(
                "p (a c) -> p a c", a=TT1),
                x_in.rearrange("b (a p) c -> b p a c", p=128)[
                    b, :, c * TT1:(c + 1) * TT1, :],
                deps=xb.wr() + bar())
            xb.setw(t_x)

            for tt in range(TT1):
                st8 = stat[:, tt * 16:tt * 16 + 16]
                t_s1 = em.step("dve", lambda e, tt=tt, st8=st8, x_sb=x_sb: e.reduce_sum(
                    st8[:, 0:1], x_sb[tt][:, :], AX.X), deps=xb.rd())
                xb.addr(t_s1)
                sqb = B("ysc")
                sqt = []
                for qq in range(4):
                    t_s2 = em.step("act", lambda e, tt=tt, st8=st8, x_sb=x_sb,
                                   qq=qq: e.activation(
                        y_sc[:, :], x_sb[tt][:, qq * 256:(qq + 1) * 256],
                        AF.Square, bias=0.0, scale=1.0,
                        accum_out=st8[:, 8 + qq:9 + qq]),
                        deps=xb.rd() + sqb.wr() + [t_s1] + bar())
                    xb.addr(t_s2)
                    sqb.setw(t_s2)
                    sqt.append(t_s2)
                t_q1 = em.step("dve", lambda e, st8=st8: e.tensor_add(
                    st8[:, 12:13], st8[:, 8:9], st8[:, 9:10]), deps=sqt)
                t_q2 = em.step("dve", lambda e, st8=st8: e.tensor_add(
                    st8[:, 13:14], st8[:, 10:11], st8[:, 11:12]), deps=sqt)
                t_s2 = em.step("dve", lambda e, st8=st8: e.tensor_add(
                    st8[:, 1:2], st8[:, 12:13], st8[:, 13:14]),
                    deps=[t_q1, t_q2])
                t_m = em.step("dve", lambda e, st8=st8: e.tensor_scalar_mul(
                    st8[:, 2:3], st8[:, 0:1], 1.0 / D), deps=[t_s1, t_s2])
                t_v1 = em.step("dve", lambda e, st8=st8: e.tensor_scalar_mul(
                    st8[:, 3:4], st8[:, 1:2], 1.0 / D), deps=[t_s2])
                t_v2 = em.step("dve", lambda e, st8=st8: e.tensor_mul(
                    st8[:, 4:5], st8[:, 2:3], st8[:, 2:3]), deps=[t_m])
                t_v3 = em.step("dve", lambda e, st8=st8: e.tensor_sub(
                    st8[:, 5:6], st8[:, 3:4], st8[:, 4:5]), deps=[t_v1, t_v2])
                t_sd = em.step("act", lambda e, st8=st8: e.activation(
                    st8[:, 6:7], st8[:, 5:6], AF.Sqrt, bias=ccols[:, 0:1], scale=1.0),
                    deps=[t_v3])
                t_rst = em.step("dve", lambda e, st8=st8: e.reciprocal(
                    st8[:, 7:8], st8[:, 6:7]), deps=[t_sd])
                xlb = B("xln0")
                t_ap = em.step("dve", lambda e, tt=tt, st8=st8, x_sb=x_sb: e.tensor_scalar(
                    xln_st[0][:, :], x_sb[tt][:, :], st8[:, 2:3],
                    st8[:, 7:8], ALU.subtract, ALU.mult),
                    deps=xb.rd() + [t_m, t_rst] + xlb.wr())
                xb.addr(t_ap)
                xlb.setw(t_ap)
                xtb = B("xlnT")
                for g4 in range(2):
                    ptr_t, ptr_n = tr_bank()
                    ptb = B(ptr_n)
                    tts = []
                    for j in range(4):
                        db = g4 * 4 + j
                        t_t = em.step("pe", lambda e, db=db, j=j, tt=tt,
                                      ptr_t=ptr_t, x_sb=x_sb: e.transpose(
                            ptr_t[:, j * 128:(j + 1) * 128],
                            xln_st[0][:, db * 128:(db + 1) * 128], eye[:, :]),
                            deps=xlb.rd() + tok_consts +
                            (ptb.wr() if j == 0 else []))
                        xlb.addr(t_t)
                        if j == 0:
                            ptb.setw(t_t)
                        tts.append(t_t)
                    dstv = xlnT.rearrange("p (db q) -> p db q", db=D // 128)[
                        :, g4 * 4:(g4 + 1) * 4, tt * 128:(tt + 1) * 128]
                    t_ev = em.step("act", lambda e, dstv=dstv, ptr_t=ptr_t: e.copy(
                        dstv, ptr_t[:, :].rearrange("p (a c) -> p a c", a=4)),
                        deps=tts + (xtb.wr() if (tt == 0 and g4 == 0) else []))
                    ptb.addr(t_ev)
                    if tt == 0 and g4 == 0:
                        xtb.setw(t_ev)
                    else:
                        xtb.addr(t_ev)
            xtb = B("xlnT")
            xlnT_tok = [xtb.w] + list(xtb.readers)

            return x_pair, x_sb, xb

        def z_slice(c, d):
            et = NDT + d
            pm_ = (p_mm0, p_mm1)[et % 2]
            pmb = B(f"pmm{et % 2}")
            xtb = B("xlnT")
            xlnT_tok = [xtb.w] + list(xtb.readers)

            def mmfn(e, et=et, pm_=pm_):
                last = None
                for k in range(D // 128):
                    last = e.matmul(pm_[:, :Q1],
                                    win[k][:, et * 128:(et + 1) * 128],
                                    xlnT[:, k * Q1:(k + 1) * Q1],
                                    start=(k == 0), stop=(k == D // 128 - 1))
                return last
            t_mm = em.step("pe", mmfn, deps=xlnT_tok + pmb.wr())
            xtb.addr(t_mm)
            pmb.setw(t_mm)
            zb = B(f"zs{d}")
            t_z = em.step("act", lambda e, d=d, pm_=pm_: e.activation(
                zs_b[:, d * Q1:(d + 1) * Q1], pm_[:, :Q1], AF.Silu,
                bias=0.0, scale=1.0), deps=pmb.rd() + zb.wr())
            pmb.addr(t_z)
            zb.setw(t_z)

        def head_b(c, x_state, n_et=2 * DI // 128):
            t0 = c * Q1
            x_pair, x_sb, xb = x_state
            xtb = B("xlnT")
            xlnT_tok = [xtb.w] + list(xtb.readers)
            for et in range(n_et):
                pm_ = (p_mm0, p_mm1)[et % 2]
                pmb = B(f"pmm{et % 2}")

                def mmfn(e, et=et, pm_=pm_):
                    last = None
                    for k in range(D // 128):
                        last = e.matmul(pm_[:, :Q1],
                                        win[k][:, et * 128:(et + 1) * 128],
                                        xlnT[:, k * Q1:(k + 1) * Q1],
                                        start=(k == 0), stop=(k == D // 128 - 1))
                    return last
                t_mm = em.step("pe", mmfn, deps=xlnT_tok + pmb.wr())
                xtb.addr(t_mm)
                pmb.setw(t_mm)

                if et < NDT:
                    d = et
                    us = u_sc[d % 2]
                    usb = B(f"usc{d % 2}")
                    t_ev = em.step("act", lambda e, us=us, pm_=pm_: e.copy(
                        us[:, DC - 1:DC - 1 + Q1], pm_[:, :Q1]),
                        deps=pmb.rd() + usb.wr())
                    pmb.addr(t_ev)
                    usb.setw(t_ev)
                    hb = B(f"uh{d}")
                    t_h = em.step("dve", lambda e, us=us, d=d: e.tensor_copy(
                        us[:, 0:DC - 1], uhalo[:, d * (DC - 1):(d + 1) * (DC - 1)]),
                        deps=hb.rd() + [t_ev])
                    hb.addr(t_h)
                    t_h2 = em.step("dve", lambda e, us=us, d=d: e.tensor_copy(
                        uhalo[:, d * (DC - 1):(d + 1) * (DC - 1)],
                        us[:, Q1:Q1 + DC - 1]), deps=[t_ev, t_h] + hb.wr())
                    hb.setw(t_h2)
                    usb.addr(t_h2)
                    cb = B("cacc")
                    cb2 = B("cacc2")
                    t_a0 = em.step("dve", lambda e, us=us, d=d: e.tensor_scalar_mul(
                        cacc[:, :], us[:, 0:Q1], convw[:, d * DC:d * DC + 1]),
                        deps=[t_ev, t_h] + cb.wr())
                    usb.addr(t_a0)
                    cb.setw(t_a0)
                    t_b0 = em.step("dve", lambda e, us=us, d=d: e.tensor_scalar_mul(
                        cacc2[:, :], us[:, 1:1 + Q1], convw[:, d * DC + 1:d * DC + 2]),
                        deps=[t_ev, t_h, t_h2] + cb2.wr())
                    usb.addr(t_b0)
                    cb2.setw(t_b0)
                    t_a1 = em.step("dve", lambda e, us=us, d=d:
                                   e.scalar_tensor_tensor(
                                       cacc[:, :], us[:, 2:2 + Q1],
                                       convw[:, d * DC + 2:d * DC + 3],
                                       cacc[:, :], ALU.mult, ALU.add),
                                   deps=[t_a0])
                    usb.addr(t_a1)
                    cb.setw(t_a1)
                    t_b1 = em.step("dve", lambda e, us=us, d=d:
                                   e.scalar_tensor_tensor(
                                       cacc2[:, :], us[:, 3:3 + Q1],
                                       convw[:, d * DC + 3:d * DC + 4],
                                       cacc2[:, :], ALU.mult, ALU.add),
                                   deps=[t_b0])
                    usb.addr(t_b1)
                    cb2.setw(t_b1)
                    t_c = em.step("dve", lambda e: e.tensor_add(
                        cacc[:, :], cacc[:, :], cacc2[:, :]),
                        deps=[t_a1, t_b1])
                    cb.setw(t_c)
                    cb2.addr(t_c)
                    ucb = B(f"uc{d}")
                    t_u = em.step("act", lambda e, d=d: e.activation(
                        uc_b[:, d * Q1:(d + 1) * Q1], cacc[:, :], AF.Silu,
                        bias=convb_c[:, d:d + 1], scale=1.0),
                        deps=[t_c] + ucb.wr())
                    cb.addr(t_u)
                    ucb.setw(t_u)
                else:
                    d = et - NDT
                    zb = B(f"zs{d}")
                    t_z = em.step("act", lambda e, d=d, pm_=pm_: e.activation(
                        zs_b[:, d * Q1:(d + 1) * Q1], pm_[:, :Q1], AF.Silu,
                        bias=0.0, scale=1.0), deps=pmb.rd() + zb.wr())
                    pmb.addr(t_z)
                    zb.setw(t_z)

            pdb = B("pdbc")

            def xpfn(e):
                last = None
                for k in range(NDT):
                    last = e.matmul(p_dbc[0:RK + 2 * DS, :Q1], wxp[k][:, :],
                                    uc_b[:, k * Q1:(k + 1) * Q1],
                                    start=(k == 0), stop=(k == NDT - 1))
                return last
            t_xp = em.step("pe", xpfn,
                           deps=sum([B(f"uc{k}").rd() for k in range(NDT)], [])
                           + pdb.wr())
            for k in range(NDT):
                B(f"uc{k}").addr(t_xp)
            pdb.setw(t_xp)
            dlb = B("dbcb")
            t_e1 = em.step("act", lambda e: e.copy(dbc_b[0:RK, :], p_dbc[0:RK, :Q1]),
                           deps=pdb.rd() + dlb.wr())
            pdb.addr(t_e1)
            t_e2 = em.step("act", lambda e: e.copy(dbc_b[64:96, :],
                                                   p_dbc[64:96, :Q1]),
                           deps=pdb.rd() + [t_e1])
            pdb.addr(t_e2)
            dlb.setw(t_e2)
            dlb.addr(t_e1)

            bb = B("bbig")
            cc = B("cbig")
            def emit_bc():
                bb = B("bbig")
                cc = B("cbig")
                i_bc = 0
                for big, base, bufo in ((bbig, 0, bb), (cbig, 16, cc)):
                    evtoks = []
                    wrdeps = bufo.wr()
                    for p2 in range(DS // 2):
                        n0 = 2 * p2
                        par = i_bc % 2
                        i_bc += 1
                        pm2 = (p_mm0, p_mm1)[par]
                        pmb2 = B(f"pmm{par}")

                        def bmfn(e, base=base, n0=n0, pm2=pm2):
                            e.matmul(pm2[:, 0:Q1],
                                     sel[64:96, (base + n0) * 128:(base + n0 + 1) * 128],
                                     dbc_b[64:96, :], start=True, stop=True)
                            return e.matmul(
                                pm2[:, Q1:2 * Q1],
                                sel[64:96, (base + n0 + 1) * 128:(base + n0 + 2) * 128],
                                dbc_b[64:96, :], start=True, stop=True)
                        t_bm = em.step("pe", bmfn, deps=dlb.rd() + pmb2.wr() + tok_consts)
                        dlb.addr(t_bm)
                        pmb2.setw(t_bm)
                        t_be = em.step("act", lambda e, big=big, n0=n0, pm2=pm2: e.copy(
                            big[:, n0 * Q1:(n0 + 2) * Q1], pm2[:, 0:2 * Q1]),
                            deps=[t_bm] + (wrdeps if n0 == 0 else []))
                        pmb2.addr(t_be)
                        evtoks.append(t_be)
                    bufo.setw(evtoks[-1])
                    for t in evtoks[:-1]:
                        bufo.addr(t)

            return emit_bc

        def dtile_loop(c, x_state, emit_bc, hooks=None):
            t0 = c * Q1
            x_pair, x_sb, xb = x_state
            dlb = B("dbcb")
            bb = B("bbig")
            cc = B("cbig")
            Xv = X_big.rearrange("p (n q) -> p n q", n=DS)
            Hv = H_big.rearrange("p (n q) -> p n q", n=DS)

            def front(d):
                dA_big = dA_bufs[d % 2]
                dtu_sc = dtu_bufs[d % 2]
                pdt_t = (p_dt, p_dbc)[d % 2]
                pdt = B(("pdt", "pdbc")[d % 2])
                t_dm = em.step("pe", lambda e, d=d, pdt_t=pdt_t: e.matmul(
                    pdt_t[:, :Q1], wdt[0:RK, d * 128:(d + 1) * 128],
                    dbc_b[0:RK, :], start=True, stop=True),
                    deps=dlb.rd() + pdt.wr())
                dlb.addr(t_dm)
                pdt.setw(t_dm)
                rb = B("dtsc")
                t_r = em.step("act", lambda e, d=d, pdt_t=pdt_t: e.activation(
                    dt_sc[:, :], pdt_t[:, :Q1], AF.Relu, bias=dtb_c[:, d:d + 1],
                    scale=1.0), deps=pdt.rd() + rb.wr())
                pdt.addr(t_r)
                rb.setw(t_r)
                ab_ = B("asc")
                t_a = em.step("act", lambda e, d=d, pdt_t=pdt_t: e.activation(
                    a_sc[:, :], pdt_t[:, :Q1], AF.Abs, bias=dtb_c[:, d:d + 1],
                    scale=1.0), deps=pdt.rd() + ab_.wr())
                pdt.addr(t_a)
                ab_.setw(t_a)
                t_a = em.step("act", lambda e: e.activation(
                    a_sc[:, :], a_sc[:, :], AF.Exp, bias=0.0, scale=-1.0),
                    deps=[t_a])
                ab_.setw(t_a)
                t_a = em.step("act", lambda e: e.activation(
                    a_sc[:, :], a_sc[:, :], AF.Ln, bias=ccols[:, 1:2], scale=1.0),
                    deps=[t_a])
                ab_.setw(t_a)
                t_dt = em.step("pool", lambda e: e.tensor_add(
                    dt_sc[:, :], dt_sc[:, :], a_sc[:, :]),
                    deps=[t_r, t_a] + rb.readers)
                rb.setw(t_dt)
                ab_.addr(t_dt)
                ucb = B(f"uc{d}")
                dub = B(f"dtu{d % 2}")
                t_du = em.step("pool", lambda e, d=d, dtu_sc=dtu_sc: e.tensor_mul(
                    dtu_sc[:, :], dt_sc[:, :], uc_b[:, d * Q1:(d + 1) * Q1]),
                    deps=[t_dt] + ucb.rd() + dub.wr())
                rb.addr(t_du)
                ucb.addr(t_du)
                dub.setw(t_du)
                dab = B(f"dAbig{d % 2}")
                datoks = []
                for n in range(DS):
                    t_da = em.step("act", lambda e, d=d, n=n, dA_big=dA_big:
                                   e.activation(
                                       dA_big[:, n * Q1:(n + 1) * Q1], dt_sc[:, :],
                                       AF.Exp, bias=0.0,
                                       scale=a_cols[:, d * DS + n:d * DS + n + 1]),
                                   deps=[t_dt] + (dab.wr() if n == 0 else []))
                    rb.addr(t_da)
                    if n == 0:
                        dab.setw(t_da)
                    datoks.append(t_da)
                return datoks

            def back(d, datoks):
                dA_big = dA_bufs[d % 2]
                dAv = dA_big.rearrange("p (n q) -> p n q", n=DS)
                dtu_sc = dtu_bufs[d % 2]
                dab = B(f"dAbig{d % 2}")
                dub = B(f"dtu{d % 2}")
                ucb = B(f"uc{d}")
                xbb = B("Xbig")
                t_X = em.step("dve", lambda e, dtu_sc=dtu_sc: e.tensor_tensor(
                    Xv, dtu_sc[:, :].unsqueeze(1).broadcast_to([128, DS, Q1]),
                    bbig[:, :].rearrange("p (n q) -> p n q", n=DS),
                    ALU.mult), deps=dub.rd() + bb.rd() + xbb.wr())
                dub.addr(t_X)
                bb.addr(t_X)
                xbb.setw(t_X)
                hc = B("hcar")
                crb = B("car16")
                t_cf1 = em.step("dve", lambda e, d=d, dAv=dAv: e.tensor_mul(
                    car16[:, :].unsqueeze(2), dAv[:, :, 0:1],
                    hcar[:, d * DS:(d + 1) * DS].unsqueeze(2)),
                    deps=datoks + hc.rd() + crb.wr())
                dab.addr(t_cf1)
                hc.addr(t_cf1)
                crb.setw(t_cf1)
                t_cf2 = em.step("dve", lambda e: e.tensor_add(
                    Xv[:, :, 0:1], Xv[:, :, 0:1], car16[:, :].unsqueeze(2)),
                    deps=[t_X, t_cf1])
                crb.addr(t_cf2)
                xbb.setw(t_cf2)
                t_cf3 = em.step("dve", lambda e, dAv=dAv: e.memset(
                    dAv[:, :, 0:1], 0.0), deps=[t_cf1])
                dab.setw(t_cf3)
                hbb = B("Hbig")
                t_s = em.step("dve", lambda e, dA_big=dA_big: e.tensor_tensor_scan(
                    H_big[:, :], dA_big[:, :], X_big[:, :], 0.0,
                    ALU.mult, ALU.add),
                    deps=datoks + [t_cf2, t_cf3] + hbb.wr())
                dab.addr(t_s)
                xbb.addr(t_s)
                hbb.setw(t_s)
                hc = B("hcar")
                t_yh = em.step("dve", lambda e: e.tensor_tensor(
                    X_big[:, :], H_big[:, :], cbig[:, :], ALU.mult),
                    deps=[t_s] + xbb.wr() + cc.rd())
                cc.addr(t_yh)
                t_hc = em.step("dve", lambda e, d=d: e.tensor_copy(
                    hcar[:, d * DS:(d + 1) * DS].unsqueeze(2),
                    Hv[:, :, Q1 - 1:Q1]), deps=[t_s] + hc.wr())
                hc.setw(t_hc)
                hbb.addr(t_hc)
                hbb.addr(t_yh)
                xbb.setw(t_yh)
                yb = B("ysc")
                h2 = 8 * Q1
                t_t1 = em.step("dve", lambda e, h2=h2: e.tensor_add(
                    tree[:, :], X_big[:, 0:h2], X_big[:, h2:2 * h2]),
                    deps=[t_yh] + hbb.wr())
                xbb.addr(t_t1)
                hbb.setw(t_t1)
                t_t2 = em.step("dve", lambda e: e.tensor_add(
                    tree[:, 0:4 * Q1], tree[:, 0:4 * Q1], tree[:, 4 * Q1:8 * Q1]),
                    deps=[t_t1])
                hbb.setw(t_t2)
                t_t3 = em.step("dve", lambda e: e.tensor_add(
                    tree[:, 0:2 * Q1], tree[:, 0:2 * Q1], tree[:, 2 * Q1:4 * Q1]),
                    deps=[t_t2])
                hbb.setw(t_t3)
                t_rd = em.step("dve", lambda e: e.tensor_add(
                    y_sc[:, :], tree[:, 0:Q1], tree[:, Q1:2 * Q1]),
                    deps=[t_t3] + yb.wr())
                hbb.addr(t_rd)
                yb.setw(t_rd)
                t_y2a = em.step("dve", lambda e, d=d: e.scalar_tensor_tensor(
                    y_sc[:, :], uc_b[:, d * Q1:(d + 1) * Q1], dp_c[:, d:d + 1],
                    y_sc[:, :], ALU.mult, ALU.add), deps=yb.wr() + ucb.rd())
                ucb.addr(t_y2a)
                yb.setw(t_y2a)
                zb = B(f"zs{d}")
                t_y2 = em.step("pool", lambda e, d=d: e.tensor_mul(
                    uc_b[:, d * Q1:(d + 1) * Q1], y_sc[:, :],
                    zs_b[:, d * Q1:(d + 1) * Q1]),
                    deps=[t_y2a] + zb.rd() + ucb.wr())
                yb.addr(t_y2)
                zb.addr(t_y2)
                ucb.setw(t_y2)

            ftoks = {0: front(0), 1: front(1)}
            emit_bc()
            for d in range(NDT):
                back(d, ftoks.pop(d))
                if d + 2 < NDT:
                    ftoks[d + 2] = front(d + 2)
                if hooks and d in hooks:
                    hooks[d]()


        def tail(c, x_state):
            t0 = c * Q1
            x_pair, x_sb, xb = x_state
            y2toks = sum([B(f"uc{k}").rd() for k in range(NDT)], [])
            res_toks = []
            for tt in range(TT1):
                if tt % 2 == 0:
                    psl = [(p_out[:, 0:512], "pout"), (p_out[:, 512:1024], "pout")]
                else:
                    psl = [(p_dbc[:, :], "pdbc"), (p_dt[:, :], "pdt")]
                pbs = [B(nm) for _, nm in psl]

                def opfn(e, tt=tt, psl=psl):
                    last = None
                    for k in range(NDT):
                        for h in range(2):
                            last = e.matmul(
                                psl[h][0],
                                uc_b[:, k * Q1 + tt * 128:k * Q1 + (tt + 1) * 128],
                                wout[k][:, h * 512:(h + 1) * 512],
                                start=(k == 0), stop=(k == NDT - 1))
                    return last
                t_op = em.step("pe", opfn, deps=y2toks +
                               sum([p.wr() for p in pbs], []))
                for k in range(NDT):
                    B(f"uc{k}").addr(t_op)
                for p in pbs:
                    p.setw(t_op)
                for h in range(2):
                    t_res = em.step("dve", lambda e, tt=tt, h=h, psl=psl,
                                    x_sb=x_sb: e.tensor_add(
                                        x_sb[tt][:, h * 512:(h + 1) * 512],
                                        x_sb[tt][:, h * 512:(h + 1) * 512],
                                        psl[h][0]),
                                    deps=xb.rd() + [t_op])
                    pbs[h].addr(t_res)
                    xb.addr(t_res)
                    res_toks.append(t_res)
            t_st = em.dma("pool", "st",
                          xmid_d.rearrange("b (a p) c -> b p a c", p=128)[
                              b, :, c * TT1:(c + 1) * TT1, :],
                          x_pair[:, :].rearrange("p (a c) -> p a c", a=TT1),
                          deps=res_toks)
            xb.addr(t_st)
            xmid_tok[b][c] = t_st

        xs = {0: head_a(0)}
        bcs = {0: head_b(0, xs[0])}
        for c in range(NCH1):
            if c + 1 < NCH1:
                xs[c + 1] = head_a(c + 1)
                hooks = {d: (lambda c=c, d=d: z_slice(c + 1, d))
                         for d in range(NDT)}
            else:
                hooks = None
            dtile_loop(c, xs[c], bcs.pop(c), hooks)
            tail(c, xs.pop(c))
            if c + 1 < NCH1:
                bcs[c + 1] = head_b(c + 1, xs[c + 1], n_et=NDT)
            barrier["v"] = []

    p2bar = {"v": em.join()}
    ar.reset(0)
    gu_b = [ar.take(HID, BF16) for _ in range(TT2)]
    xm_pairs = [ar.take(TT2 * D, F32) for _ in range(2)]
    x2_sc = ar.take(D, F32)
    xq_bf = [ar.take(D, BF16) for _ in range(TT2)]
    xqTs = [ar.take((D // 128) * Q2, BF16) for _ in range(2)]
    xq2_i = ar.take(HID, I32)
    xq2_bf = ar.take(HID, BF16)
    xq2T = ar.take((HID // 128) * Q2, BF16)
    wg_sb = [ar.take(8 * 512, BF16) for _ in range(2)]
    wu_sb = [ar.take(8 * 512, BF16) for _ in range(2)]
    wd_sb = [ar.take(512, BF16) for _ in range(8)]
    gate_s = [ar.take(512, BF16) for _ in range(TT2)]
    assert ar.off <= ar.cols, ar.off

    def p2b():
        return p2bar["v"]

    def p2_front(b, c):
        par = c % 2
        xm_pair = xm_pairs[par]
        xm_sb = [xm_pair[:, tt * D:(tt + 1) * D] for tt in range(TT2)]
        xqT = xqTs[par]
        xmb = B(f"p2xch{par}")
        t_xl = em.dma("sp", "ld", xm_pair[:, :].rearrange(
            "p (a c) -> p a c", a=TT2),
            xmid_d.rearrange("b (a p) c -> b p a c", p=128)[
                b, :, c * TT2:(c + 1) * TT2, :],
            deps=xmb.wr() + [xmid_tok[b][c]] + p2b())
        xmb.setw(t_xl)
        xtb2 = B(f"xqT{par}")
        for tt in range(TT2):
            st8 = stat[:, tt * 8:tt * 8 + 8]
            t_s1 = em.step("dve", lambda e, tt=tt, st8=st8, xm_sb=xm_sb:
                           e.reduce_sum(st8[:, 0:1], xm_sb[tt][:, :], AX.X),
                           deps=xmb.rd() + p2b())
            xmb.addr(t_s1)
            sqb = B("p2sq")
            t_s2 = em.step("act", lambda e, tt=tt, st8=st8, xm_sb=xm_sb:
                           e.activation(
                               x2_sc[:, :], xm_sb[tt][:, :], AF.Square, bias=0.0,
                               scale=1.0, accum_out=st8[:, 1:2]),
                           deps=xmb.rd() + sqb.wr() + [t_s1] + p2b())
            xmb.addr(t_s2)
            sqb.setw(t_s2)
            t_m = em.step("dve", lambda e, st8=st8: e.tensor_scalar_mul(
                st8[:, 2:3], st8[:, 0:1], 1.0 / D), deps=[t_s1, t_s2])
            t_v1 = em.step("dve", lambda e, st8=st8: e.tensor_scalar_mul(
                st8[:, 3:4], st8[:, 1:2], 1.0 / D), deps=[t_s2])
            t_v2 = em.step("dve", lambda e, st8=st8: e.tensor_mul(
                st8[:, 4:5], st8[:, 2:3], st8[:, 2:3]), deps=[t_m])
            t_v3 = em.step("dve", lambda e, st8=st8: e.tensor_sub(
                st8[:, 5:6], st8[:, 3:4], st8[:, 4:5]), deps=[t_v1, t_v2])
            t_sd = em.step("act", lambda e, st8=st8: e.activation(
                st8[:, 6:7], st8[:, 5:6], AF.Sqrt, bias=ccols[:, 0:1], scale=1.0),
                deps=[t_v3])
            t_rst = em.step("dve", lambda e, st8=st8: e.reciprocal(
                st8[:, 7:8], st8[:, 6:7]), deps=[t_sd])
            x2b = B("p2x2sc")
            t_x2 = em.step("dve", lambda e, tt=tt, st8=st8, xm_sb=xm_sb:
                           e.tensor_scalar(
                               x2_sc[:, :], xm_sb[tt][:, :], st8[:, 2:3],
                               st8[:, 7:8], ALU.subtract, ALU.mult),
                           deps=xmb.rd() + [t_m, t_rst, t_s2] + x2b.wr() + sqb.wr())
            xmb.addr(t_x2)
            x2b.setw(t_x2)
            sqb.setw(t_x2)
            sc8 = statb[:, par * 16 + tt * 4:par * 16 + tt * 4 + 4]
            scb = B(f"sc{par}{tt}")
            t_am = em.step("dve", lambda e, sc8=sc8: e.reduce_max(
                sc8[:, 0:1], x2_sc[:, :], AX.X, apply_absolute_value=True),
                deps=[t_x2] + scb.wr())
            x2b.addr(t_am)
            scb.setw(t_am)
            t_si = em.step("dve", lambda e, sc8=sc8: e.tensor_scalar(
                sc8[:, 1:2], sc8[:, 0:1], 1e-5, 1.0 / 127.0, ALU.max, ALU.mult),
                deps=[t_am])
            scb.setw(t_si)
            t_sq = em.step("dve", lambda e, sc8=sc8: e.reciprocal(
                sc8[:, 2:3], sc8[:, 1:2]), deps=[t_si])
            scb.setw(t_sq)
            t_sg = em.step("dve", lambda e, sc8=sc8: e.tensor_mul(
                sc8[:, 3:4], sc8[:, 1:2], cm_c[:, 0:1]),
                deps=[t_si] + B("cm").rd())
            B("cm").addr(t_sg)
            scb.setw(t_sg)
            t_su = em.step("dve", lambda e, sc8=sc8, tt=tt, par=par: e.tensor_mul(
                stat[:, 32 + par * 4 + tt:33 + par * 4 + tt], sc8[:, 1:2],
                cm_c[:, 1:2]), deps=[t_si] + B("cm").rd())
            B("cm").addr(t_su)
            scb.setw(t_su)
            qib = B("p2qi")
            t_qi = em.step("act", lambda e, sc8=sc8: e.activation(
                xq2_i[:, :D], x2_sc[:, :], AF.Copy, bias=0.0,
                scale=sc8[:, 2:3]), deps=[t_x2, t_sq] + qib.wr())
            x2b.addr(t_qi)
            scb.addr(t_qi)
            qib.setw(t_qi)
            qbb = B(f"xqbf{tt}")
            t_qb = em.step("dve", lambda e, tt=tt: e.tensor_copy(
                xq_bf[tt][:, :], xq2_i[:, :D]), deps=[t_qi] + qbb.wr())
            qib.addr(t_qb)
            qbb.setw(t_qb)
            for g4 in range(2):
                ptr_t, ptr_n = tr_bank()
                ptb = B(ptr_n)
                tts = []
                for j in range(4):
                    db = g4 * 4 + j
                    t_t = em.step("pe", lambda e, db=db, j=j, tt=tt,
                                  ptr_t=ptr_t: e.transpose(
                        ptr_t[:, j * 128:(j + 1) * 128],
                        xq_bf[tt][:, db * 128:(db + 1) * 128], eye[:, :]),
                        deps=qbb.rd() + (ptb.wr() if j == 0 else []) + p2b())
                    qbb.addr(t_t)
                    if j == 0:
                        ptb.setw(t_t)
                    tts.append(t_t)
                dstv = xqT.rearrange("p (db q) -> p db q", db=D // 128)[
                    :, g4 * 4:(g4 + 1) * 4, tt * 128:(tt + 1) * 128]
                t_ev = em.step("act", lambda e, dstv=dstv, ptr_t=ptr_t: e.copy(
                    dstv, ptr_t[:, :].rearrange("p (a c) -> p a c", a=4)),
                    deps=tts + (xtb2.wr() if (tt == 0 and g4 == 0) else []))
                ptb.addr(t_ev)
                if tt == 0 and g4 == 0:
                    xtb2.setw(t_ev)
                else:
                    xtb2.addr(t_ev)
        p2bar["v"] = []

    def p2_mid(b, c):
        par = c % 2
        xqT = xqTs[par]
        xtb2 = B(f"xqT{par}")
        xqT_tok = [xtb2.w] + list(xtb2.readers)
        for s in range(HID // 512):
            wgb = B(f"wg{s % 2}")
            wub = B(f"wu{s % 2}")
            tg = []
            tu = []
            for k in range(8):
                tg.append(em.dma("sp", "ld", wg_sb[s % 2][:, k * 512:(k + 1) * 512],
                                 wqt_d["g"][k * 128:(k + 1) * 128,
                                            s * 512:(s + 1) * 512],
                                 deps=(wgb.wr() if k == 0 else [])))
                tu.append(em.dma("sp", "ld", wu_sb[s % 2][:, k * 512:(k + 1) * 512],
                                 wqt_d["u"][k * 128:(k + 1) * 128,
                                            s * 512:(s + 1) * 512],
                                 deps=(wub.wr() if k == 0 else [])))
            wgb.setw(tg[-1])
            for t in tg[:-1]:
                wgb.addr(t)
            wub.setw(tu[-1])
            for t in tu[:-1]:
                wub.addr(t)
            for tt in range(TT2):
                sc8 = statb[:, par * 16 + tt * 4:par * 16 + tt * 4 + 4]
                pg_t = (p_mm0, p_dbc)[tt % 2]
                pg = B(("pmm0", "pdbc")[tt % 2])
                t_g = em.step("pe", lambda e, s=s, tt=tt, pg_t=pg_t, xqT=xqT: _gmm(
                    e, pg_t, xqT, wg_sb[s % 2], tt),
                    deps=xqT_tok + tg + pg.wr())
                xtb2.addr(t_g)
                wgb.addr(t_g)
                pg.setw(t_g)
                gsb = B(f"gs{tt}")
                t_sil = em.step("act", lambda e, tt=tt, sc8=sc8,
                                pg_t=pg_t: e.activation(
                    gate_s[tt][:, :], pg_t[:, :], AF.Silu, bias=0.0,
                    scale=sc8[:, 3:4]), deps=pg.rd() + gsb.wr() +
                    B(f"sc{par}{tt}").rd())
                pg.addr(t_sil)
                B(f"sc{par}{tt}").addr(t_sil)
                gsb.setw(t_sil)
                pu_t = (p_mm1, p_dt)[tt % 2]
                pu = B(("pmm1", "pdt")[tt % 2])
                t_u = em.step("pe", lambda e, s=s, tt=tt, pu_t=pu_t, xqT=xqT: _gmm(
                    e, pu_t, xqT, wu_sb[s % 2], tt),
                    deps=xqT_tok + tu + pu.wr())
                xtb2.addr(t_u)
                wub.addr(t_u)
                pu.setw(t_u)
                gub = B(f"gu{tt}")
                t_gu = em.step("dve", lambda e, s=s, tt=tt, pu_t=pu_t,
                               par=par: e.scalar_tensor_tensor(
                    gu_b[tt][:, s * 512:(s + 1) * 512], pu_t[:, :],
                    stat[:, 32 + par * 4 + tt:33 + par * 4 + tt],
                    gate_s[tt][:, :], ALU.mult, ALU.mult),
                    deps=pu.rd() + [t_sil] + B(f"sc{par}{tt}").rd() +
                    (gub.wr() if s == 0 else []))
                pu.addr(t_gu)
                gsb.addr(t_gu)
                B(f"sc{par}{tt}").addr(t_gu)
                if s == 0:
                    gub.setw(t_gu)
                else:
                    gub.addr(t_gu)
        for tt in range(TT2):
            gub = B(f"gu{tt}")
            gutoks = gub.rd() + list(gub.readers)
            sc8 = statb[:, 32 + tt * 4:32 + tt * 4 + 4]
            scb = B(f"sc2{tt}")
            t_am = em.step("dve", lambda e, tt=tt, sc8=sc8: e.reduce_max(
                sc8[:, 0:1], gu_b[tt][:, :], AX.X, apply_absolute_value=True),
                deps=gutoks + scb.wr())
            gub.addr(t_am)
            scb.setw(t_am)
            t_si = em.step("dve", lambda e, sc8=sc8: e.tensor_scalar(
                sc8[:, 1:2], sc8[:, 0:1], 1e-5, 1.0 / 127.0, ALU.max, ALU.mult),
                deps=[t_am])
            scb.setw(t_si)
            t_sq = em.step("dve", lambda e, sc8=sc8: e.reciprocal(
                sc8[:, 2:3], sc8[:, 1:2]), deps=[t_si])
            scb.setw(t_sq)
            t_sd2 = em.step("dve", lambda e, sc8=sc8: e.tensor_mul(
                sc8[:, 3:4], sc8[:, 1:2], cm_c[:, 2:3]),
                deps=[t_si] + B("cm").rd())
            B("cm").addr(t_sd2)
            scb.setw(t_sd2)
            qib = B("p2qi")
            t_qi = em.step("act", lambda e, tt=tt, sc8=sc8: e.activation(
                xq2_i[:, :], gu_b[tt][:, :], AF.Copy, bias=0.0,
                scale=sc8[:, 2:3]), deps=gutoks + [t_sq] + qib.wr())
            gub.addr(t_qi)
            scb.addr(t_qi)
            qib.setw(t_qi)
            qbb = B("xq2bf")
            t_qb = em.step("dve", lambda e: e.tensor_copy(
                xq2_bf[:, :], xq2_i[:, :]), deps=[t_qi] + qbb.wr())
            qib.addr(t_qb)
            qbb.setw(t_qb)
            xtb3 = B("xq2T")
            for g8 in range(8):
                ptr_t, ptr_n = tr_bank()
                ptb = B(ptr_n)
                tts = []
                for j in range(4):
                    ib = g8 * 4 + j
                    t_t = em.step("pe", lambda e, ib=ib, j=j,
                                  ptr_t=ptr_t: e.transpose(
                        ptr_t[:, j * 128:(j + 1) * 128],
                        xq2_bf[:, ib * 128:(ib + 1) * 128], eye[:, :]),
                        deps=qbb.rd() + (ptb.wr() if j == 0 else []))
                    qbb.addr(t_t)
                    if j == 0:
                        ptb.setw(t_t)
                    tts.append(t_t)
                dstv = xq2T.rearrange("p (ib q) -> p ib q", ib=HID // 128)[
                    :, g8 * 4:(g8 + 1) * 4, tt * 128:(tt + 1) * 128]
                t_ev = em.step("act", lambda e, dstv=dstv, ptr_t=ptr_t: e.copy(
                    dstv, ptr_t[:, :].rearrange("p (a c) -> p a c", a=4)),
                    deps=tts + (xtb3.wr() if (tt == 0 and g8 == 0) else []))
                ptb.addr(t_ev)
                if tt == 0 and g8 == 0:
                    xtb3.setw(t_ev)
                else:
                    xtb3.addr(t_ev)

    def p2_down(b, c):
        par = c % 2
        t0 = c * Q2
        xm_pair = xm_pairs[par]
        xm_sb = [xm_pair[:, tt * D:(tt + 1) * D] for tt in range(TT2)]
        xmb = B(f"p2xch{par}")
        xtb3 = B("xq2T")
        xq2T_tok = [xtb3.w] + list(xtb3.readers)
        psums = (p_mm0, p_mm1, p_dbc, p_dt)
        pnames = ("pmm0", "pmm1", "pdbc", "pdt")
        for dmh in range(2):
            for k in range(HID // 128):
                wdb = B(f"wd{k % 8}")
                t_w = em.dma("sp", "ld", wd_sb[k % 8][:, :],
                             wqt_d["d"][k * 128:(k + 1) * 128,
                                        dmh * 512:(dmh + 1) * 512],
                             deps=wdb.wr())
                wdb.setw(t_w)

                def dnfn(e, k=k, dmh=dmh):
                    last = None
                    for tt in range(TT2):
                        last = e.matmul(
                            psums[tt][:, :],
                            xq2T[:, k * Q2 + tt * 128:k * Q2 + (tt + 1) * 128],
                            wd_sb[k % 8][:, :],
                            start=(k == 0), stop=(k == HID // 128 - 1))
                    return last
                pd = [B(p) for p in pnames]
                t_dn = em.step("pe", dnfn,
                               deps=xq2T_tok + wdb.rd() +
                               (sum([p.wr() for p in pd], []) if k == 0 else []))
                wdb.addr(t_dn)
                xtb3.addr(t_dn)
                if k == 0:
                    for p in pd:
                        p.setw(t_dn)
                else:
                    for p in pd:
                        p.addr(t_dn)
            for tt in range(TT2):
                sc8 = statb[:, 32 + tt * 4:32 + tt * 4 + 4]
                pb_ = B(pnames[tt])
                xb = B(f"p2x{par}{tt}")
                t_r = em.step("dve", lambda e, tt=tt, dmh=dmh, sc8=sc8,
                              xm_sb=xm_sb: e.scalar_tensor_tensor(
                                  xm_sb[tt][:, dmh * 512:(dmh + 1) * 512],
                                  psums[tt][:, :], sc8[:, 3:4],
                                  xm_sb[tt][:, dmh * 512:(dmh + 1) * 512],
                                  ALU.mult, ALU.add),
                              deps=pb_.wr() + xmb.rd() + B(f"sc2{tt}").rd() +
                              xb.wr())
                pb_.addr(t_r)
                B(f"sc2{tt}").addr(t_r)
                xmb.addr(t_r)
                xb.setw(t_r)
        st_deps = sum([B(f"p2x{par}{t_}").rd() for t_ in range(TT2)], [])
        t_st = em.dma("pool", "st",
                      out_d.rearrange("b (a p) c -> b p a c", p=128)[
                          b, :, c * TT2:(c + 1) * TT2, :],
                      xm_pair[:, :].rearrange("p (a c) -> p a c", a=TT2),
                      deps=st_deps)
        for t_ in range(TT2):
            B(f"p2x{par}{t_}").addr(t_st)
        xmb.addr(t_st)

    for b in range(NB):
        p2_front(b, 0)
        for c in range(NCH2):
            p2_mid(b, c)
            if c + 1 < NCH2:
                p2_front(b, c + 1)
            p2_down(b, c)

    em.step("pool", lambda e: e.nop(), deps=[("st", em.dcnt["st"]),
                                             ("ld", em.dcnt["ld"])] + em.join())

    import contextlib
    with contextlib.ExitStack() as es:
        sems = {}
        for s in ENGS:
            sems[s] = es.enter_context(nc.semaphore(f"sem_{s}"))
        for s in DMAS:
            sems[s] = es.enter_context(nc.semaphore(f"sem_{s}"))
        with nc.Block() as block:
            em.emit(block, sems)
    return nc


def _gmm(e, psum, xqT, w_sb, tt):
    last = None
    for k in range(8):
        last = e.matmul(psum[:, :], xqT[:, k * Q2 + tt * 128:k * Q2 + (tt + 1) * 128],
                        w_sb[:, k * 512:(k + 1) * 512],
                        start=(k == 0), stop=(k == 7))
    return last


_CONSTS = None


def _consts():
    global _CONSTS
    if _CONSTS is None:
        sel = np.zeros((32, 32 * 128), dtype=ml_dtypes.bfloat16)
        for n in range(32):
            sel[n, n * 128:(n + 1) * 128] = 1.0
        _CONSTS = {
            "c_eye": np.eye(128, dtype=ml_dtypes.bfloat16),
            "c_sel": sel,
            "c_onesrow": np.ones((1, 128), dtype=np.float32),
            "c_onesb": np.ones((128, 1), dtype=ml_dtypes.bfloat16),
            "c_cols": np.concatenate([np.full((128, 1), 1e-5, np.float32),
                                      np.ones((128, 1), np.float32)], axis=1),
        }
    return _CONSTS


def kernel(**inputs):
    x = np.ascontiguousarray(inputs["L0_attn_x"])
    weights = {k: np.ascontiguousarray(v) for k, v in inputs.items()
               if k != "L0_attn_x"}
    nc = build_program()
    consts = _consts()
    in_maps = []
    for c in range(8):
        m = {"L0_attn_x": x[2 * c:2 * c + 2]}
        m.update(weights)
        m.update(consts)
        in_maps.append(m)
    res = run_bass_kernel_spmd(nc, in_maps, core_ids=list(range(8)))
    return np.concatenate([r["out"] for r in res.results], axis=0)
